# version 04
# baseline (speedup 1.0000x reference)
import numpy as np
import ml_dtypes
from contextlib import ExitStack

import concourse.bass as bass
import concourse.mybir as mybir
from concourse.bass_utils import run_bass_kernel_spmd

F32 = mybir.dt.float32
BF16 = mybir.dt.bfloat16
AF = mybir.ActivationFunctionType
ALU = mybir.AluOpType
AX = mybir.AxisListType
bfnp = ml_dtypes.bfloat16

D = 2048
T = 2048
EI = 8512
DI = 4096
NH = 64
CH = 128
TT = 512
NTT = T // TT
CPT = TT // CH
NKD = D // 128
E = 8
I_FF = 1408
NIP = 11
TT2 = 1024
NTT2 = T // TT2
EPS = 1e-5

ENGINES = ("sync", "gpsimd", "vector", "scalar", "tensor")


def build_nc(race_detect=True, debug=False, phases=(0, 1, 2, 3), p1sub=99, p2sub=99):
    nc = bass.Bass("TRN2", target_bir_lowering=False, detect_race_conditions=race_detect)

    d_uT = nc.dram_tensor("uT", [D, T], F32, kind="ExternalInput")
    d_WinT = nc.dram_tensor("WinT", [D, EI], F32, kind="ExternalInput")
    d_convW = nc.dram_tensor("convW", [128, 136], F32, kind="ExternalInput")
    d_convB = nc.dram_tensor("convB", [128, 34], F32, kind="ExternalInput")
    d_dtb = nc.dram_tensor("dtb", [64, 1], F32, kind="ExternalInput")
    d_Alog = nc.dram_tensor("Alog", [64, 1], F32, kind="ExternalInput")
    d_DCOL = nc.dram_tensor("DCOL", [128, 32], F32, kind="ExternalInput")
    d_nw = nc.dram_tensor("nwc", [128, 32], F32, kind="ExternalInput")
    d_WoutT = nc.dram_tensor("WoutT", [DI, D], F32, kind="ExternalInput")
    d_rw = nc.dram_tensor("rw", [128, 128], F32, kind="ExternalInput")
    d_w1 = nc.dram_tensor("w1p", [E, D, 2816], F32, kind="ExternalInput")
    d_w2 = nc.dram_tensor("w2p", [E, I_FF, D], F32, kind="ExternalInput")
    d_IDBF = nc.dram_tensor("idbf", [128, 128], BF16, kind="ExternalInput")
    d_IDF32 = nc.dram_tensor("idf32", [128, 128], F32, kind="ExternalInput")
    d_TRIU = nc.dram_tensor("triu", [128, 128], F32, kind="ExternalInput")
    d_ONESR = nc.dram_tensor("onesr", [1, 128], BF16, kind="ExternalInput")
    d_ONESC = nc.dram_tensor("onesc", [128, 1], F32, kind="ExternalInput")
    d_ONESCB = nc.dram_tensor("onescb", [128, 1], BF16, kind="ExternalInput")
    d_ONES64 = nc.dram_tensor("ones64", [64, 128], F32, kind="ExternalInput")

    d_out = nc.dram_tensor("outT", [D, T], F32, kind="ExternalOutput")

    d_xn = nc.dram_tensor("xn_s", [D, T], BF16)
    d_g = nc.dram_tensor("g_s", [E, T], F32)
    d_attn = nc.dram_tensor("attn_s", [D, T], F32)
    d_acs = nc.dram_tensor("acs_s", [2, 64, 128], BF16)
    d_diag = nc.dram_tensor("diag_s", [34, 128, 512], BF16)
    d_silz = nc.dram_tensor("silz_s", [32, 128, TT], BF16)
    d_dbg = nc.dram_tensor("dbg_s", [16, 128, DI], BF16) if debug else None
    d_dbg2 = nc.dram_tensor("dbg2_s", [16, 128, DI], BF16) if debug else None
    d_dbg3 = nc.dram_tensor("dbg3_s", [16, 128, 256], F32) if debug else None
    d_dbg4 = nc.dram_tensor("dbg4_s", [16, 128, 128], F32) if debug else None
    d_dbg5 = nc.dram_tensor("dbg5_s", [16, 1, 16384], BF16) if debug else None
    d_dbg6 = nc.dram_tensor("dbg6_s", [16, 2, 128, 128], BF16) if debug else None
    d_dbg7 = nc.dram_tensor("dbg7_s", [16, 128, 128], F32) if debug else None

    streams = {e: [] for e in ENGINES}
    cnt = {}
    sems = {}

    skip = [False]

    def op(eng_name, f, inc=None, inc_n=1):
        if skip[0]:
            return
        if inc is not None:
            cnt[inc] += inc_n
            streams[eng_name].append(
                lambda eng, f=f, s=inc, n=inc_n: f(eng).then_inc(sems[s], n))
        else:
            streams[eng_name].append(lambda eng, f=f: f(eng))

    def wait(eng_name, sem_name, v=None):
        if skip[0]:
            return
        if v is None:
            v = cnt[sem_name]
        streams[eng_name].append(
            lambda eng, s=sem_name, v=v: eng.wait_ge(sems[s], v))

    SEM_NAMES = ["lc", "lu0", "lu1", "lx", "wA", "wB", "wC", "sA", "sB",
                 "ac0", "ac1", "sz0", "sz1", "gg", "ua0", "ua1", "so",
                 "pe", "ve", "se", "ge", "bar", "dgo", "mwr", "wk"]
    for s in SEM_NAMES:
        cnt[s] = 0

    def barrier():
        if skip[0]:
            return
        for e in ENGINES:
            streams[e].append(lambda eng: eng.nop().then_inc(sems["bar"], 1))
            cnt["bar"] += 1
        tgt = cnt["bar"]
        for e in ENGINES:
            wait(e, "bar", tgt)

    with ExitStack() as ctx:
        def sb(name, shape, dt):
            return ctx.enter_context(nc.sbuf_tensor(name, shape, dt))

        def psum(name, shape, dt):
            return ctx.enter_context(nc.psum_tensor(name, shape, dt))

        for s in SEM_NAMES:
            sems[s] = ctx.enter_context(nc.semaphore(s))

        PB = [psum(f"pb{i}", [128, 512], F32) for i in range(6)]
        PT = [psum(f"pt{i}", [128, 128], BF16) for i in range(2)]

        c_IDBF = sb("cIDBF", [128, 128], BF16)
        c_IDF32 = sb("cIDF32", [128, 128], F32)
        c_TRIU = sb("cTRIU", [128, 128], F32)
        c_ONESR = sb("cONESR", [1, 128], BF16)
        c_ONESC = sb("cONESC", [128, 1], F32)
        c_ONESCB = sb("cONESCB", [128, 1], BF16)
        c_rwh = sb("crwh", [128, 128], BF16)
        c_rwl = sb("crwl", [128, 128], BF16)
        c_rwstg = sb("crwstg", [128, 128], F32)
        c_ONES64 = sb("cONES64", [64, 128], F32)
        c_convW = sb("cconvW", [128, 136], F32)
        c_convB = sb("cconvB", [128, 34], F32)
        c_dtb = sb("cdtb", [64, 1], F32)
        c_Alog = sb("cAlog", [64, 1], F32)
        c_DCOL = sb("cDCOL", [128, 32], F32)
        c_nw = sb("cnw", [128, 32], F32)
        c_rw = sb("crw", [128, 128], F32)
        c_Acol = sb("cAcol", [64, 1], F32)
        c_cpool = sb("cpool", [128, 2], F32)
        nc.const_aps.register(F32, EPS, c_cpool[:, 0:1])
        nc.const_aps.register(F32, 1.0, c_cpool[:, 1:2])
        dstg = sb("dstg", [128, 1024], BF16)

        for (dst, src) in [(c_IDBF, d_IDBF), (c_IDF32, d_IDF32), (c_TRIU, d_TRIU),
                           (c_ONESR, d_ONESR), (c_ONESC, d_ONESC), (c_ONESCB, d_ONESCB),
                           (c_ONES64, d_ONES64),
                           (c_convW, d_convW), (c_convB, d_convB), (c_dtb, d_dtb),
                           (c_Alog, d_Alog), (c_DCOL, d_DCOL), (c_nw, d_nw), (c_rw, d_rw)]:
            op("sync", lambda e, a=dst, b=src: e.dma_start(out=a[:], in_=b[:]), inc="lc", inc_n=16)
        wait("scalar", "lc")
        op("scalar", lambda e: e.activation(c_rwh[:], c_rw[:], AF.Copy), inc="se")
        op("scalar", lambda e: e.activation(c_Acol[:], c_Alog[:], AF.Exp), inc="se")
        wait("vector", "se")
        op("vector", lambda e: e.tensor_copy(c_rwstg[:], c_rwh[:]), inc="ve")
        wait("gpsimd", "ve")
        op("gpsimd", lambda e: e.tensor_sub(c_rwl[:], c_rw[:], c_rwstg[:]), inc="ge")
        wait("vector", "lc")
        wait("vector", "se")
        op("vector", lambda e: e.tensor_scalar(c_Acol[:], c_Acol[:], -1.0, None, ALU.mult), inc="ve")
        op("vector", lambda e: e.memset(c_cpool[:, 0:1], EPS))
        op("vector", lambda e: e.memset(c_cpool[:, 1:2], 1.0))
        for rt in range(34):
            sl = rt % 2
            if rt >= 2:
                wait("vector", "dgo")
            for k in range(4):
                op("vector",
                   lambda e, rt=rt, k=k, sl=sl: e.tensor_scalar(
                       dstg[:, sl * 512 + k * 128:sl * 512 + (k + 1) * 128], c_IDBF[:],
                       c_convW[:, rt * 4 + k:rt * 4 + k + 1], None, ALU.mult),
                   inc=("ve" if k == 3 else None))
            wait("sync", "ve")
            op("sync", lambda e, rt=rt, sl=sl: e.dma_start(
                out=d_diag[rt], in_=dstg[:, sl * 512:(sl + 1) * 512]), inc="dgo", inc_n=16)
        wait("sync", "dgo")
        barrier()

        skip[0] = 1 not in phases
        p1_base_skip = [skip[0]]

        def p1s(n):
            skip[0] = p1_base_skip[0] or (n > p1sub)
        with ExitStack() as p1:
            def sb1(name, shape, dt):
                return p1.enter_context(nc.sbuf_tensor(name, shape, dt))
            u2 = [sb1(f"p1u{i}", [128, TT], F32) for i in range(2)]
            usq = [sb1(f"p1usq{i}", [128, TT], F32) for i in range(2)]
            ubf = [sb1(f"p1ub{i}", [128, TT], BF16) for i in range(2)]
            usqbf = [sb1(f"p1uqb{i}", [128, TT], BF16) for i in range(2)]
            xnf = [sb1(f"p1xnf{k}", [128, TT], F32) for k in range(NKD)]
            t1b = sb1("p1t1", [128, TT], F32)
            t2b = sb1("p1t2", [128, TT], F32)
            xnh = [sb1(f"p1xnh{k}", [128, TT], BF16) for k in range(NKD)]
            xnl = [sb1(f"p1xnl{k}", [128, TT], BF16) for k in range(NKD)]
            rowst = sb1("p1rows", [1, 6 * TT], F32)
            rowbf = sb1("p1rowbf", [1, 2 * TT], BF16)
            lg = sb1("p1lg", [128, 32], F32)
            m1c = sb1("p1m1", [128, 4], F32)
            m2c = sb1("p1m2", [128, 4], F32)
            msk1 = sb1("p1msk1", [128, 32], F32)
            msk2 = sb1("p1msk2", [128, 32], F32)
            mskd = sb1("p1mskd", [128, 32], F32)
            d1t = sb1("p1d1t", [128, 32], F32)
            sg1 = sb1("p1sg1", [128, 32], F32)
            t5t = sb1("p1t5t", [128, 32], F32)
            w1c = sb1("p1w1", [128, 4], F32)
            w2c = sb1("p1w2", [128, 4], F32)
            ecol = sb1("p1e", [128, 4], F32)
            ecx = sb1("p1ecx", [128, 4], F32)
            gtm = sb1("p1gtm", [128, 32], F32)
            gtm2 = sb1("p1gtm2", [128, 32], F32)
            gstage = sb1("p1gst", [32, 128], F32)

            P_SUM, P_SQ, P_MBC, P_RBC, P_LG, P_GT = PB[0], PB[1], PB[2], PB[3], PB[4], PB[5]
            R = rowst

            for tt in range(NTT):
                p1s(10)
                for k in range(NKD):
                    sl = u2[k % 2]
                    lus = "lu0" if k % 2 == 0 else "lu1"
                    if k >= 2:
                        wait("sync", "ve")
                    op("sync", lambda e, sl=sl, k=k, tt=tt: e.dma_start(
                        out=sl[:], in_=d_uT[k * 128:(k + 1) * 128, tt * TT:(tt + 1) * TT]),
                        inc=lus, inc_n=16)
                    wait("vector", lus)
                    if k >= 2:
                        wait("vector", "wk")
                    op("vector", lambda e, sl=sl, k=k: e.tensor_mul(usq[k % 2][:], sl[:], sl[:]))
                    op("vector", lambda e, sl=sl, k=k: e.tensor_copy(ubf[k % 2][:], sl[:]), inc="ve")
                    wait("gpsimd", "ve")
                    op("gpsimd", lambda e, k=k: e.tensor_copy(usqbf[k % 2][:], usq[k % 2][:]), inc="ge")
                    wait("tensor", "ge")
                    op("tensor", lambda e, k=k: e.matmul(
                        P_SUM[:1, :], c_ONESCB[:], ubf[k % 2][:], start=(k == 0), stop=(k == NKD - 1)))
                    op("tensor", lambda e, k=k: e.matmul(
                        P_SQ[:1, :], c_ONESCB[:], usqbf[k % 2][:], start=(k == 0), stop=(k == NKD - 1)),
                        inc=("pe" if k == NKD - 1 else "wk"))
                p1s(20)
                wait("vector", "pe")
                op("vector", lambda e: e.tensor_scalar(R[:, 0:TT], P_SUM[:1, :], 1.0 / D, None, ALU.mult))
                op("vector", lambda e: e.tensor_scalar(R[:, TT:2 * TT], P_SQ[:1, :], 1.0 / D, None, ALU.mult))
                op("vector", lambda e: e.tensor_mul(R[:, 5 * TT:6 * TT], R[:, 0:TT], R[:, 0:TT]))
                op("vector", lambda e: e.tensor_sub(R[:, 2 * TT:3 * TT], R[:, TT:2 * TT], R[:, 5 * TT:6 * TT]), inc="ve")
                p1s(21)
                wait("scalar", "ve")
                op("scalar", lambda e: e.activation(R[:, 3 * TT:4 * TT], R[:, 2 * TT:3 * TT], AF.Ln, bias=EPS), inc="se")
                p1s(22)
                wait("vector", "se")
                op("vector", lambda e: e.tensor_scalar(R[:, 5 * TT:6 * TT], R[:, 3 * TT:4 * TT], -0.5, None, ALU.mult), inc="ve")
                wait("scalar", "ve")
                op("scalar", lambda e: e.activation(R[:, 4 * TT:5 * TT], R[:, 5 * TT:6 * TT], AF.Exp), inc="se")
                p1s(23)
                wait("vector", "se")
                op("vector", lambda e: e.tensor_copy(rowbf[:, 0:TT], R[:, 0:TT]))
                op("vector", lambda e: e.tensor_copy(rowbf[:, TT:2 * TT], R[:, 4 * TT:5 * TT]), inc="ve")
                wait("tensor", "ve")
                op("tensor", lambda e: e.matmul(P_MBC[:], c_ONESR[:], rowbf[0:1, 0:TT], start=True, stop=True))
                op("tensor", lambda e: e.matmul(P_RBC[:], c_ONESR[:], rowbf[0:1, TT:2 * TT], start=True, stop=True), inc="pe")
                p1s(30)
                wait("sync", "pe")
                for k in range(NKD):
                    sl = u2[k % 2]
                    lus = "lu0" if k % 2 == 0 else "lu1"
                    if k >= 2:
                        wait("sync", "ve")
                    op("sync", lambda e, sl=sl, k=k, tt=tt: e.dma_start(
                        out=sl[:], in_=d_uT[k * 128:(k + 1) * 128, tt * TT:(tt + 1) * TT]),
                        inc=lus, inc_n=16)
                    wait("vector", lus)
                    op("vector", lambda e, sl=sl: e.tensor_sub(t1b[:], sl[:], P_MBC[:]))
                    op("vector", lambda e, k=k: e.tensor_mul(xnf[k][:], t1b[:], P_RBC[:]))
                    op("vector", lambda e, k=k: e.tensor_copy(xnh[k][:], xnf[k][:]), inc="ve")
                    wait("gpsimd", "ve")
                    op("gpsimd", lambda e, k=k: e.tensor_copy(t2b[:], xnh[k][:]), inc="ge")
                    wait("vector", "ge")
                    op("vector", lambda e, k=k: e.tensor_sub(xnl[k][:], xnf[k][:], t2b[:]), inc="ve")
                    wait("sync", "ve")
                    op("sync", lambda e, k=k, tt=tt: e.dma_start(
                        out=d_xn[k * 128:(k + 1) * 128, tt * TT:(tt + 1) * TT], in_=xnh[k][:]),
                        inc="mwr", inc_n=16)
                p1s(40)
                wait("tensor", "ve")
                for sub in range(TT // 128):
                    for k in range(NKD):
                        op("tensor", lambda e, sub=sub, k=k: e.matmul(
                            P_LG[:, sub * 8:(sub + 1) * 8], xnh[k][:, sub * 128:(sub + 1) * 128],
                            c_rwh[:, k * 8:(k + 1) * 8], start=(k == 0), stop=False))
                        op("tensor", lambda e, sub=sub, k=k: e.matmul(
                            P_LG[:, sub * 8:(sub + 1) * 8], xnh[k][:, sub * 128:(sub + 1) * 128],
                            c_rwl[:, k * 8:(k + 1) * 8], start=False, stop=False))
                        op("tensor", lambda e, sub=sub, k=k: e.matmul(
                            P_LG[:, sub * 8:(sub + 1) * 8], xnl[k][:, sub * 128:(sub + 1) * 128],
                            c_rwh[:, k * 8:(k + 1) * 8], start=False, stop=(k == NKD - 1)),
                            inc=("pe" if (k == NKD - 1 and sub == 3) else None))
                p1s(50)
                def bc4(t):
                    return bass.AP(t if not hasattr(t, "tensor") else t.tensor, 0, [[4, 128], [1, 4], [0, 8]])
                def seg(t):
                    return t[:].rearrange("p (s e) -> p s e", s=4)
                def col4(t):
                    return t[:].rearrange("p (s o) -> p s o", o=1)
                wait("scalar", "pe")
                op("scalar", lambda e: e.activation(lg[:], P_LG[:, 0:32], AF.Copy), inc="se")
                wait("vector", "se")
                op("vector", lambda e: e.reduce_max(col4(m1c), seg(lg), axis=AX.X), inc="ve")
                wait("gpsimd", "ve")
                op("gpsimd", lambda e: e.tensor_tensor(seg(d1t), seg(lg), bc4(m1c), ALU.subtract), inc="ge")
                wait("scalar", "ge")
                op("scalar", lambda e: e.activation(sg1[:], d1t[:], AF.Sign), inc="se")
                wait("gpsimd", "se")
                op("gpsimd", lambda e: e.tensor_scalar(msk1[:], sg1[:], 1.0, None, ALU.add), inc="ge")
                wait("vector", "ge")
                op("vector", lambda e: e.tensor_scalar(t5t[:], msk1[:], -1e30, None, ALU.mult), inc="ve")
                wait("gpsimd", "ve")
                op("gpsimd", lambda e: e.tensor_add(mskd[:], t5t[:], lg[:]), inc="ge")
                wait("vector", "ge")
                op("vector", lambda e: e.reduce_max(col4(m2c), seg(mskd), axis=AX.X), inc="ve")
                wait("gpsimd", "ve")
                op("gpsimd", lambda e: e.tensor_tensor(seg(d1t), seg(mskd), bc4(m2c), ALU.subtract), inc="ge")
                wait("scalar", "ge")
                op("scalar", lambda e: e.activation(sg1[:], d1t[:], AF.Sign), inc="se")
                wait("vector", "se")
                op("vector", lambda e: e.tensor_scalar(msk2[:], sg1[:], 1.0, None, ALU.add))
                op("vector", lambda e: e.tensor_sub(ecol[:], m2c[:], m1c[:]), inc="ve")
                wait("scalar", "ve")
                op("scalar", lambda e: e.activation(ecx[:], ecol[:], AF.Exp), inc="se")
                wait("gpsimd", "se")
                op("gpsimd", lambda e: e.tensor_scalar(w2c[:], ecx[:], 1.0, None, ALU.add), inc="ge")
                wait("vector", "ge")
                op("vector", lambda e: e.reciprocal(w1c[:], w2c[:]), inc="ve")
                wait("gpsimd", "ve")
                op("gpsimd", lambda e: e.tensor_mul(w2c[:], ecx[:], w1c[:]), inc="ge")
                wait("vector", "ge")
                op("vector", lambda e: e.tensor_tensor(seg(gtm2), seg(msk2), bc4(w2c), ALU.mult), inc="ve")
                wait("gpsimd", "ve")
                op("gpsimd", lambda e: e.tensor_tensor(seg(gtm), seg(msk1), bc4(w1c), ALU.mult), inc="ge")
                wait("vector", "ge")
                op("vector", lambda e: e.tensor_add(gtm[:], gtm[:], gtm2[:]), inc="ve")
                p1s(60)
                wait("tensor", "ve")
                op("tensor", lambda e: e.transpose(P_GT[0:32, 0:128], gtm[:], c_IDF32[:]), inc="pe")
                wait("vector", "pe")
                op("vector", lambda e: e.tensor_copy(gstage[:], P_GT[0:32, 0:128]), inc="ve")
                wait("sync", "ve")
                for sub in range(4):
                    op("sync", lambda e, tt=tt, sub=sub: e.dma_start(
                        out=d_g[:, tt * TT + sub * 128:tt * TT + (sub + 1) * 128],
                        in_=gstage[sub * 8:(sub + 1) * 8, :]), inc="gg", inc_n=16)
            p1s(90)
            skip[0] = p1_base_skip[0]
            wait("sync", "gg")
            wait("sync", "mwr")
            barrier()

        skip[0] = 2 not in phases
        p2_base_skip = [skip[0]]

        def p2s(n):
            skip[0] = p2_base_skip[0] or (n > p2sub)
        with ExitStack() as p2:
            def sb2(name, shape, dt):
                return p2.enter_context(nc.sbuf_tensor(name, shape, dt))
            xnc = [sb2(f"p2xn{k}", [128, TT], BF16) for k in range(NKD)]
            wp = [sb2(f"p2wp{i}", [128, 768], BF16) for i in range(3)]
            pre = [sb2(f"p2pre{i}", [128, 516], BF16) for i in range(6)]
            halo = sb2("p2halo", [128, 3 * 34], BF16)
            dgt = [sb2(f"p2dg{i}", [128, 512], BF16) for i in range(2)]
            cvo = [sb2(f"p2cv{i}", [128, TT], BF16) for i in range(34)]
            szst = [sb2(f"p2sz{i}", [128, 32 * 128], BF16) for i in range(2)]
            dtf = sb2("p2dtf", [64, TT], F32)
            sp1 = sb2("p2sp1", [64, TT], F32)
            dAf = sb2("p2dAf", [64, 128], F32)
            acsf = sb2("p2acs", [64, 128], F32)
            acshi = sb2("p2acshi", [64, 128], BF16)
            acshif = sb2("p2acshif", [64, 128], F32)
            acslo = sb2("p2acslo", [64, 128], BF16)
            apack = sb2("p2apack", [1, 16384], BF16)
            negat = sb2("p2negat", [128, 64], F32)
            dttm = sb2("p2dttm", [128, 64], F32)
            wexp = sb2("p2wexp", [128, 64], F32)
            alrow = sb2("p2alrow", [1, 64], F32)
            dkrow = sb2("p2dkrow", [1, 192], F32)
            dkbf = sb2("p2dkbf", [1, 128], BF16)
            dkbc = sb2("p2dkbc", [128, 64], F32)
            mscr = sb2("p2mscr", [128, 128], F32)
            d12 = sb2("p2d12", [128, 128], F32)
            expd = sb2("p2expd", [128, 128], F32)
            eabf = [sb2(f"p2ea{i}", [128, 128], BF16) for i in range(2)]
            mtil = [sb2(f"p2mt{i}", [128, 128], BF16) for i in range(2)]
            ctil = [sb2(f"p2ct{i}", [128, 128], BF16) for i in range(2)]
            xw = [sb2(f"p2xw{i}", [128, 64], BF16) for i in range(2)]
            xtm = sb2("p2xtm", [128, DI], BF16)
            btm = sb2("p2btm", [128, 128], BF16)
            S = sb2("p2S", [128, DI], F32)
            Sbf = sb2("p2Sbf", [128, DI], BF16)
            sstg = sb2("p2sstg", [128, 64], F32)
            yg = sb2("p2yg", [128, DI], BF16)
            ytmp = sb2("p2ytmp", [128, 128], F32)
            ytmp2 = sb2("p2ytmp2", [128, 128], F32)
            ygsq = [sb2(f"p2ygsq{i}", [128, 128], BF16) for i in range(2)]
            rrow = sb2("p2rrow", [1, 128], F32)
            rrow2 = sb2("p2rrow2", [1, 128], F32)
            rbfr = sb2("p2rbfr", [1, 128], BF16)
            ynorm = [sb2(f"p2yn{i}", [128, TT], BF16) for i in range(32)]
            astg = [sb2(f"p2as{i}", [128, 512], F32) for i in range(1)]

            for i8 in range(8):
                op("vector", lambda e, i8=i8: e.memset(S[:, i8 * 512:(i8 + 1) * 512], 0.0),
                   inc=("ve" if i8 == 7 else None))

            NE_TILE = 67
            pidx = [0]
            wslot = [0, 0, 0]

            def wsem(i):
                return ("wA", "wB", "wC")[i]

            def stream_piece(dram, k, c0, ncols):
                i = pidx[0]
                sl = i % 3
                if i >= 3:
                    wait("gpsimd", "wk", cnt["wk"] - 2)
                op("gpsimd", lambda e, sl=sl, k=k, c0=c0, ncols=ncols, dram=dram: e.dma_start(
                    out=wp[sl][:, 0:ncols], in_=dram[k * 128:(k + 1) * 128, c0:c0 + ncols]),
                    inc=wsem(sl), inc_n=16)
                wslot[sl] += 1
                wait("tensor", wsem(sl), 16 * wslot[sl])
                pidx[0] += 1
                return sl

            for tt in range(NTT):
                wait("sync", "wk")
                wait("sync", "pe")
                for k in range(NKD):
                    op("sync", lambda e, k=k, tt=tt: e.dma_start(
                        out=xnc[k][:], in_=d_xn[k * 128:(k + 1) * 128, tt * TT:(tt + 1) * TT]),
                        inc="lx", inc_n=16)
                wait("tensor", "lx")

                groups = []
                et = 0
                while et < NE_TILE:
                    groups.append(list(range(et, min(et + 6, NE_TILE))))
                    et += 6
                prslot = {}
                pre_ring = [0]
                conv_ctr = [0]
                for gi, g in enumerate(groups):
                    p2s(11)
                    wait("tensor", "ve")
                    wait("tensor", "se")
                    for k in range(NKD):
                        c0 = g[0] * 128
                        ncols = sum(64 if e_ == 66 else 128 for e_ in g)
                        sl = stream_piece(d_WinT, k, c0, ncols)
                        col = 0
                        for j, e_ in enumerate(g):
                            w_ = 64 if e_ == 66 else 128
                            tp = (0, 0)
                            op("tensor", lambda e, j=j, sl=sl, col=col, w_=w_, k=k, tp=tp: e.matmul(
                                PB[j][0:w_, :] if w_ == 64 else PB[j][:, :],
                                wp[sl][:, col:col + w_], xnc[k][:],
                                start=(k == 0), stop=(k == NKD - 1), tile_position=tp),
                                inc=("wk" if j == len(g) - 1 else None))
                            col += w_
                    p2s(12)
                    wait("vector", "wk")
                    wait("scalar", "wk")
                    for j, e_ in enumerate(g):
                        if e_ < 32:
                            ssl = e_ % 2
                            if e_ >= 2:
                                wait("scalar", "sz0" if ssl == 0 else "sz1")
                            op("scalar", lambda e, j=j, ssl=ssl: e.activation(
                                dstg[:, ssl * 512:(ssl + 1) * 512], PB[j][:], AF.Silu), inc="se")
                            wait("sync", "se")
                            op("sync", lambda e, e_=e_, ssl=ssl: e.dma_start(
                                out=d_silz[e_], in_=dstg[:, ssl * 512:(ssl + 1) * 512]),
                                inc=("sz0" if ssl == 0 else "sz1"), inc_n=16)
                        elif e_ < 66:
                            rt = e_ - 32
                            ps_ = pre_ring[0] % 6
                            prslot[rt] = ps_
                            pre_ring[0] += 1
                            pr = pre[ps_]
                            if tt == 0:
                                op("vector", lambda e, pr=pr: e.memset(pr[:, 0:3], 0.0))
                            else:
                                op("vector", lambda e, pr=pr, rt=rt: e.tensor_copy(
                                    pr[:, 0:3], halo[:, rt * 3:(rt + 1) * 3]))
                            op("vector", lambda e, pr=pr, j=j: e.tensor_copy(pr[:, 3:515], PB[j][:]))
                            op("vector", lambda e, j=j, rt=rt: e.tensor_copy(
                                halo[:, rt * 3:(rt + 1) * 3], PB[j][:, 509:512]), inc="ve")
                        else:
                            op("vector", lambda e, j=j: e.tensor_copy(dtf[0:64, :], PB[j][0:64, :]), inc="ve")
                    p2s(13)
                    for j, e_ in enumerate(g):
                        if not (32 <= e_ < 66):
                            continue
                        rt = e_ - 32
                        di_ = conv_ctr[0] % 2
                        conv_ctr[0] += 1
                        dcs = "ac0" if di_ == 0 else "ac1"
                        if conv_ctr[0] > 2:
                            wait("sync", "pe")
                        op("sync", lambda e, rt=rt, di_=di_: e.dma_start(
                            out=dgt[di_][:], in_=d_diag[rt]), inc=dcs, inc_n=16)
                        wait("tensor", dcs)
                        wait("tensor", "ve")
                        for kk in range(4):
                            op("tensor", lambda e, j=j, di_=di_, kk=kk, rt=rt: e.matmul(
                                PB[j][:], dgt[di_][:, kk * 128:(kk + 1) * 128],
                                pre[prslot[rt]][:, kk:kk + 512],
                                start=(kk == 0), stop=(kk == 3)),
                                inc=("pe" if kk == 3 else None))
                        wait("scalar", "pe")
                        op("scalar", lambda e, rt=rt, j=j: e.activation(
                            cvo[rt][:], PB[j][:], AF.Silu, bias=c_convB[:, rt:rt + 1]), inc="se")

                p2s(2)
                wait("vector", "ve")
                op("vector", lambda e: e.tensor_scalar(dtf[0:64, :], dtf[0:64, :], c_dtb[:], None, ALU.add), inc="ve")
                wait("scalar", "ve")
                op("scalar", lambda e: e.activation(sp1[:], dtf[0:64, :], AF.Abs))
                op("scalar", lambda e: e.activation(sp1[:], sp1[:], AF.Exp, scale=-1.0))
                op("scalar", lambda e: e.activation(sp1[:], sp1[:], AF.Ln, bias=1.0), inc="se")
                wait("vector", "se")
                op("vector", lambda e: e.tensor_scalar(dtf[0:64, :], dtf[0:64, :], 0.0, None, ALU.max))
                op("vector", lambda e: e.tensor_add(dtf[0:64, :], dtf[0:64, :], sp1[:]), inc="ve")

                p2s(3)
                for cc in range(CPT):
                    p2s(3)
                    col0 = cc * 128
                    ssl = szst[cc % 2]
                    wait("sync", "ve")
                    wait("sync", "sz0")
                    wait("sync", "sz1")
                    szs = "lu0" if cc % 2 == 0 else "lu1"
                    for q in range(4):
                        op("sync", lambda e, ssl=ssl, q=q, col0=col0: e.dma_start(
                            out=ssl[:, q * 8 * 128:(q + 1) * 8 * 128].rearrange("p (t c) -> p t c", t=8),
                            in_=d_silz[q * 8:(q + 1) * 8, :, col0:col0 + 128].rearrange("t p c -> p t c")),
                            inc=szs, inc_n=16)
                    wait("gpsimd", "ve")
                    op("gpsimd", lambda e, col0=col0: e.tensor_scalar(
                        dAf[:], dtf[0:64, col0:col0 + 128], c_Acol[:], None, ALU.mult), inc="ge")
                    wait("vector", "ge")
                    op("vector", lambda e: e.tensor_tensor_scan(
                        acsf[:], c_ONES64[:], dAf[:], 0.0, ALU.mult, ALU.add), inc="ve")
                    wait("scalar", "ve")
                    op("scalar", lambda e: e.activation(acshi[:], acsf[:], AF.Copy), inc="se")
                    wait("vector", "se")
                    op("vector", lambda e: e.tensor_copy(acshif[:], acshi[:]), inc="ve")
                    wait("gpsimd", "ve")
                    op("gpsimd", lambda e: e.tensor_sub(acslo[:], acsf[:], acshif[:]), inc="ge")
                    wait("vector", "ge")
                    wait("sync", "ve")
                    wait("sync", "se")
                    wait("sync", "ge")
                    wait("sync", "pe")
                    wait("sync", "ac1")
                    op("sync", lambda e: e.dma_start(out=d_acs[0], in_=acshi[:]), inc="ac0", inc_n=16)
                    op("sync", lambda e: e.dma_start(out=d_acs[1], in_=acslo[:]), inc="ac0", inc_n=16)
                    wait("sync", "ac0")
                    op("sync", lambda e: e.dma_start(
                        out=apack[:], in_=d_acs[:].rearrange("a h l -> (a h) l").rearrange("(o x) l -> o (x l)", o=1)),
                        inc="ac1", inc_n=16)
                    wait("tensor", "ve")
                    wait("tensor", "se")
                    op("tensor", lambda e: e.transpose(PB[0][0:128, 0:64], acsf[:], c_IDF32[0:64, 0:64]), inc="pe")
                    wait("vector", "pe")
                    op("vector", lambda e: e.tensor_scalar(negat[:], PB[0][:, 0:64], -1.0, None, ALU.mult), inc="ve")
                    wait("tensor", "ve")
                    op("tensor", lambda e, col0=col0: e.transpose(
                        PB[0][0:128, 64:128], dtf[0:64, col0:col0 + 128], c_IDF32[0:64, 0:64]), inc="pe")
                    wait("vector", "pe")
                    op("vector", lambda e: e.tensor_copy(dttm[:], PB[0][:, 64:128]), inc="ve")
                    wait("tensor", "se")
                    wait("tensor", "ve")
                    op("tensor", lambda e, col0=col0: e.transpose(
                        PT[0][:], cvo[32][:, col0:col0 + 128], c_IDBF[:]), inc="pe")
                    wait("vector", "pe")
                    op("vector", lambda e: e.tensor_copy(btm[:], PT[0][:]), inc="ve")
                    for xt_ in range(32):
                        wait("tensor", "ve")
                        op("tensor", lambda e, xt_=xt_, col0=col0: e.transpose(
                            PT[xt_ % 2][:], cvo[xt_][:, col0:col0 + 128], c_IDBF[:]), inc="pe")
                        wait("vector", "pe")
                        op("vector", lambda e, xt_=xt_: e.tensor_copy(
                            xtm[:, xt_ * 128:(xt_ + 1) * 128], PT[xt_ % 2][:]), inc="ve")
                    op("tensor", lambda e, col0=col0: e.matmul(
                        PB[1][:, 0:128], cvo[32][:, col0:col0 + 128], cvo[33][:, col0:col0 + 128],
                        start=True, stop=True), inc="pe")
                    wait("vector", "pe")
                    op("vector", lambda e: e.tensor_mul(mscr[:], PB[1][:, 0:128], c_TRIU[:]), inc="ve")
                    for i8 in range(8):
                        op("vector", lambda e, i8=i8: e.tensor_copy(
                            Sbf[:, i8 * 512:(i8 + 1) * 512], S[:, i8 * 512:(i8 + 1) * 512]),
                            inc=("ve" if i8 == 7 else None))
                    wait("vector", "ac1")
                    op("vector", lambda e: e.tensor_add(alrow[:], apack[0:1, 127:8192:128], apack[0:1, 8319:16384:128]))
                    op("vector", lambda e: e.tensor_copy(dkbf[0:1, 0:64], apack[0:1, 127:8192:128]))
                    op("vector", lambda e: e.tensor_copy(dkbf[0:1, 64:128], apack[0:1, 8319:16384:128]), inc="ve")
                    wait("tensor", "ve")
                    op("tensor", lambda e: e.matmul(PB[0][:, 128:192], c_ONESR[:], dkbf[0:1, 0:64], start=True, stop=False))
                    op("tensor", lambda e: e.matmul(PB[0][:, 128:192], c_ONESR[:], dkbf[0:1, 64:128], start=False, stop=True), inc="pe")
                    wait("vector", "pe")
                    op("vector", lambda e: e.tensor_add(wexp[:], PB[0][:, 128:192], negat[:]), inc="ve")
                    wait("scalar", "ve")
                    op("scalar", lambda e: e.activation(wexp[:], wexp[:], AF.Exp))
                    op("scalar", lambda e: e.activation(dkrow[0:1, 0:64], alrow[:], AF.Exp), inc="se")
                    wait("vector", "se")
                    op("vector", lambda e: e.tensor_mul(wexp[:], wexp[:], dttm[:]))
                    op("vector", lambda e: e.tensor_copy(dkbf[0:1, 0:64], dkrow[0:1, 0:64]), inc="ve")
                    wait("scalar", "ve")
                    op("scalar", lambda e: e.activation(dkrow[0:1, 64:128], dkbf[0:1, 0:64], AF.Copy), inc="se")
                    wait("gpsimd", "se")
                    op("gpsimd", lambda e: e.tensor_sub(dkrow[0:1, 128:192], dkrow[0:1, 0:64], dkrow[0:1, 64:128]), inc="ge")
                    wait("vector", "ge")
                    op("vector", lambda e: e.tensor_copy(dkbf[0:1, 64:128], dkrow[0:1, 128:192]), inc="ve")
                    wait("tensor", "ve")
                    op("tensor", lambda e: e.matmul(PB[0][:, 192:256], c_ONESR[:], dkbf[0:1, 0:64], start=True, stop=False))
                    op("tensor", lambda e: e.matmul(PB[0][:, 192:256], c_ONESR[:], dkbf[0:1, 64:128], start=False, stop=True), inc="pe")
                    wait("vector", "pe")
                    op("vector", lambda e: e.tensor_copy(dkbc[:], PB[0][:, 192:256]), inc="ve")

                    p2s(4)
                    wait("vector", "lu0" if cc % 2 == 0 else "lu1")
                    for hg in range(16):
                        wait("tensor", "se")
                        wait("tensor", "ve")
                        op("tensor", lambda e, hg=hg: e.matmul(
                            PB[3][:], c_ONESR[:], apack[0:1, hg * 512:(hg + 1) * 512], start=True, stop=False))
                        op("tensor", lambda e, hg=hg: e.matmul(
                            PB[3][:], c_ONESR[:], apack[0:1, 8192 + hg * 512:8192 + (hg + 1) * 512], start=False, stop=True), inc="pe")
                        wait("vector", "pe")
                        wait("scalar", "pe")
                        for hh in range(4):
                            h = hg * 4 + hh
                            par = h % 2
                            pair = h // 2
                            ybank = PB[2] if pair % 2 == 0 else PB[5]
                            op("vector", lambda e, hh=hh, h=h: e.tensor_scalar(
                                d12[:], PB[3][:, hh * 128:(hh + 1) * 128], negat[:, h:h + 1], 0.0, ALU.add, ALU.min), inc="ve")
                            wait("scalar", "ve")
                            op("scalar", lambda e: e.activation(expd[:], d12[:], AF.Exp))
                            op("scalar", lambda e, hh=hh, par=par: e.activation(
                                eabf[par][:], PB[3][:, hh * 128:(hh + 1) * 128], AF.Exp), inc="se")
                            wait("vector", "se")
                            op("vector", lambda e, h=h, par=par: e.scalar_tensor_tensor(
                                mtil[par][:], expd[:], dttm[:, h:h + 1], mscr[:], ALU.mult, ALU.mult))
                            op("vector", lambda e, par=par, col0=col0: e.tensor_mul(
                                ctil[par][:], cvo[33][:, col0:col0 + 128], eabf[par][:]))
                            op("vector", lambda e, h=h, par=par: e.tensor_scalar(
                                xw[par][:], xtm[:, h * 64:(h + 1) * 64], wexp[:, h:h + 1], None, ALU.mult), inc="ve")
                            if debug and h < 2:
                                wait("sync", "ve")
                                op("sync", lambda e, tt=tt, cc=cc, h=h, par=par: e.dma_start(
                                    out=d_dbg6[tt * CPT + cc, h], in_=mtil[par][:]), inc="dgo", inc_n=16)
                                wait("vector", "dgo")
                            wait("tensor", "ve")
                            tp = (0, 0) if par == 0 else (0, 64)
                            pr0, pr1 = (0, 64) if par == 0 else (64, 128)
                            if debug != "off":
                                op("tensor", lambda e, h=h, par=par, yb=ybank, tp=tp, pr0=pr0, pr1=pr1: e.matmul(
                                    yb[pr0:pr1, 0:128], xtm[:, h * 64:(h + 1) * 64], mtil[par][:],
                                    start=True, stop=(debug == "diag"), tile_position=tp),
                                    inc=("pe" if debug == "diag" else None))
                            if debug != "diag":
                                op("tensor", lambda e, h=h, par=par, yb=ybank, tp=tp, pr0=pr0, pr1=pr1: e.matmul(
                                    yb[pr0:pr1, 0:128], Sbf[:, h * 64:(h + 1) * 64], ctil[par][:],
                                    start=(debug == "off"), stop=True, tile_position=tp), inc="pe")
                            op("tensor", lambda e, h=h, par=par: e.matmul(
                                PB[4][:, (h % 8) * 64:(h % 8) * 64 + 64], btm[:], xw[par][:],
                                start=True, stop=True), inc="pe")
                            wait("scalar", "pe")
                            op("scalar", lambda e, h=h: e.activation(
                                sstg[:], S[:, h * 64:(h + 1) * 64], AF.Copy, scale=dkbc[:, h:h + 1]), inc="se")
                            wait("vector", "se")
                            wait("vector", "pe")
                            op("vector", lambda e, h=h: e.tensor_add(
                                S[:, h * 64:(h + 1) * 64], sstg[:], PB[4][:, (h % 8) * 64:(h % 8) * 64 + 64]), inc="ve")
                            if par == 1:
                                wait("scalar", "ve")
                                op("scalar", lambda e, pair=pair, col0=col0: e.activation(
                                    ytmp[:], cvo[pair][:, col0:col0 + 128], AF.Copy,
                                    scale=c_DCOL[:, pair:pair + 1]), inc="se")
                                wait("vector", "se")
                                op("vector", lambda e, yb=ybank: e.tensor_add(ytmp2[:], ytmp[:], yb[:, 0:128]), inc="ve")
                                wait("gpsimd", "ve")
                                op("gpsimd", lambda e, ssl=ssl, pair=pair: e.tensor_mul(
                                    yg[:, pair * 128:(pair + 1) * 128], ytmp2[:], ssl[:, pair * 128:(pair + 1) * 128]), inc="ge")
                                if debug and pair == 0:
                                    op("vector", lambda e, yb=ybank: e.tensor_copy(ytmp[:], yb[:, 0:128]), inc="ve")
                                    wait("sync", "ve")
                                    op("sync", lambda e, tt=tt, cc=cc: e.dma_start(
                                        out=d_dbg7[tt * CPT + cc], in_=ytmp[:]), inc="dgo", inc_n=16)
                                    wait("vector", "dgo")
                    if debug:
                        wait("sync", "ge")
                        wait("sync", "ve")
                        ci = tt * CPT + cc
                        op("sync", lambda e, ci=ci: e.dma_start(out=d_dbg[ci], in_=yg[:]), inc="dgo", inc_n=16)
                        op("sync", lambda e, ci=ci: e.dma_start(out=d_dbg2[ci], in_=xtm[:]), inc="dgo", inc_n=16)
                        op("sync", lambda e, ci=ci: e.dma_start(out=d_dbg3[ci][:, 0:64], in_=negat[:]), inc="dgo", inc_n=16)
                        op("sync", lambda e, ci=ci: e.dma_start(out=d_dbg3[ci][:, 64:128], in_=dttm[:]), inc="dgo", inc_n=16)
                        op("sync", lambda e, ci=ci: e.dma_start(out=d_dbg3[ci][:, 128:192], in_=wexp[:]), inc="dgo", inc_n=16)
                        op("sync", lambda e, ci=ci: e.dma_start(out=d_dbg3[ci][:, 192:256], in_=dkbc[:]), inc="dgo", inc_n=16)
                        op("sync", lambda e, ci=ci: e.dma_start(out=d_dbg4[ci], in_=mscr[:]), inc="dgo", inc_n=16)
                        op("sync", lambda e, ci=ci: e.dma_start(out=d_dbg5[ci], in_=apack[:]), inc="dgo", inc_n=16)
                        wait("vector", "dgo")
                        wait("tensor", "dgo")
                    p2s(5)
                    wait("vector", "ge")
                    for t32 in range(32):
                        if t32 >= 2:
                            wait("vector", "wk")
                        op("vector", lambda e, t32=t32: e.tensor_mul(
                            ygsq[t32 % 2][:], yg[:, t32 * 128:(t32 + 1) * 128], yg[:, t32 * 128:(t32 + 1) * 128]), inc="ve")
                        wait("tensor", "ve")
                        op("tensor", lambda e, t32=t32: e.matmul(
                            PB[0][0:1, 256:384], c_ONESCB[:], ygsq[t32 % 2][:],
                            start=(t32 == 0), stop=(t32 == 31)), inc=("pe" if t32 == 31 else "wk"))
                    wait("scalar", "pe")
                    op("scalar", lambda e: e.activation(rrow[:], PB[0][0:1, 256:384], AF.Ln, scale=1.0 / DI, bias=EPS), inc="se")
                    wait("vector", "se")
                    op("vector", lambda e: e.tensor_scalar(rrow2[:], rrow[:], -0.5, None, ALU.mult), inc="ve")
                    wait("scalar", "ve")
                    op("scalar", lambda e: e.activation(rrow[:], rrow2[:], AF.Exp), inc="se")
                    wait("gpsimd", "se")
                    op("gpsimd", lambda e: e.tensor_copy(rbfr[:], rrow[:]), inc="ge")
                    wait("tensor", "ge")
                    op("tensor", lambda e: e.matmul(PB[1][:, 128:256], c_ONESR[:], rbfr[:], start=True, stop=True), inc="pe")
                    wait("vector", "pe")
                    for t32 in range(32):
                        op("vector", lambda e, t32=t32, col0=col0: e.scalar_tensor_tensor(
                            ynorm[t32][:, col0:col0 + 128], yg[:, t32 * 128:(t32 + 1) * 128],
                            c_nw[:, t32:t32 + 1], PB[1][:, 128:256], ALU.mult, ALU.mult),
                            inc=("ve" if t32 == 31 else None))

                p2s(6)
                wgrp = [list(range(0, 6)), list(range(6, 12)), list(range(12, 16))]
                for gi, g in enumerate(wgrp):
                    c0 = g[0] * 128
                    ncols = len(g) * 128
                    wait("tensor", "ve")
                    wait("tensor", "se")
                    for k in range(32):
                        sl = stream_piece(d_WoutT, k, c0, ncols)
                        for j, dm in enumerate(g):
                            op("tensor", lambda e, j=j, sl=sl, k=k: e.matmul(
                                PB[j][:], wp[sl][:, j * 128:(j + 1) * 128], ynorm[k][:],
                                start=(k == 0), stop=(k == 31)),
                                inc=("wk" if j == len(g) - 1 else None))
                    wait("vector", "wk")
                    for j, dm in enumerate(g):
                        asl = astg[0]
                        uasem = "ua0"
                        wait("vector", uasem)
                        op("vector", lambda e, j=j, asl=asl: e.tensor_copy(asl[:], PB[j][:]), inc="ve")
                        wait("sync", "ve")
                        op("sync", lambda e, dm=dm, tt=tt, asl=asl: e.dma_start(
                            out=d_attn[dm * 128:(dm + 1) * 128, tt * TT:(tt + 1) * TT], in_=asl[:]),
                            inc=uasem, inc_n=16)
            p2s(99)
            skip[0] = p2_base_skip[0]
            wait("sync", "ua0")
            wait("sync", "ua1")
            barrier()

        skip[0] = 3 not in phases
        with ExitStack() as p3:
            def sb3(name, shape, dt):
                return p3.enter_context(nc.sbuf_tensor(name, shape, dt))
            xnc3 = [sb3(f"p3xn{k}", [128, TT2], BF16) for k in range(NKD)]
            ffn = [sb3(f"p3f{i}", [128, TT2], F32) for i in range(16)]
            act = [sb3(f"p3a{i}", [128, TT2], BF16) for i in range(NIP)]
            w1s = sb3("p3w1s", [128, 16 * 512], BF16)
            w2s = sb3("p3w2s", [128, 11 * 256], BF16)
            gst = sb3("p3gst", [1, TT2], F32)
            gbf = sb3("p3gbf", [1, TT2], BF16)
            gbc = sb3("p3gbc", [128, TT2], F32)
            silt = sb3("p3silt", [128, 512], F32)
            t2s = sb3("p3t2", [128, 512], F32)
            ustg = [sb3(f"p3u{i}", [128, TT2], F32) for i in range(2)]
            astg3 = [sb3(f"p3at{i}", [128, TT2], F32) for i in range(2)]

            for t2 in range(NTT2):
                wait("sync", "pe")
                for k in range(NKD):
                    op("sync", lambda e, k=k, t2=t2: e.dma_start(
                        out=xnc3[k][:], in_=d_xn[k * 128:(k + 1) * 128, t2 * TT2:(t2 + 1) * TT2]),
                        inc="lx", inc_n=16)
                for dm in range(16):
                    us, as_ = ustg[dm % 2], astg3[dm % 2]
                    uasem = "ua0" if dm % 2 == 0 else "ua1"
                    if dm >= 2:
                        wait("sync", "ve")
                    op("sync", lambda e, dm=dm, t2=t2, us=us: e.dma_start(
                        out=us[:], in_=d_uT[dm * 128:(dm + 1) * 128, t2 * TT2:(t2 + 1) * TT2]),
                        inc=uasem, inc_n=16)
                    op("sync", lambda e, dm=dm, t2=t2, as_=as_: e.dma_start(
                        out=as_[:], in_=d_attn[dm * 128:(dm + 1) * 128, t2 * TT2:(t2 + 1) * TT2]),
                        inc=uasem, inc_n=16)
                    wait("vector", uasem)
                    op("vector", lambda e, dm=dm, us=us, as_=as_: e.tensor_add(ffn[dm][:], us[:], as_[:]), inc="ve")
                wait("tensor", "lx")
                for e_ in range(E):
                    if not (t2 == 0 and e_ == 0):
                        wait("sync", "ve")
                    op("sync", lambda e, e_=e_, t2=t2: e.dma_start(
                        out=gst[:], in_=d_g[e_:e_ + 1, t2 * TT2:(t2 + 1) * TT2]), inc="gg", inc_n=16)
                    wait("vector", "gg")
                    op("vector", lambda e: e.tensor_copy(gbf[:], gst[:]), inc="ve")
                    wait("tensor", "ve")
                    wait("tensor", "pe", cnt["pe"])
                    op("tensor", lambda e: e.matmul(PB[0][:], c_ONESR[:], gbf[0:1, 0:512], start=True, stop=True))
                    op("tensor", lambda e: e.matmul(PB[1][:], c_ONESR[:], gbf[0:1, 512:1024], start=True, stop=True), inc="pe")
                    wait("vector", "pe")
                    op("vector", lambda e: e.tensor_copy(gbc[:, 0:512], PB[0][:]))
                    op("vector", lambda e: e.tensor_copy(gbc[:, 512:1024], PB[1][:]), inc="ve")
                    bank_rot = [0]
                    for pg2 in range(6):
                        np_ = 2 if pg2 < 5 else 1
                        cw = np_ * 256
                        wait("gpsimd", "pe")
                        for q in range(4):
                            op("gpsimd", lambda e, e_=e_, pg2=pg2, q=q, cw=cw: e.dma_start(
                                out=w1s[:, q * 4 * cw:(q + 1) * 4 * cw].rearrange("p (k c) -> p k c", k=4),
                                in_=d_w1[e_, q * 512:(q + 1) * 512, pg2 * 512:pg2 * 512 + cw].rearrange(
                                    "(k p) c -> p k c", p=128)),
                                inc="sA", inc_n=16)
                        wait("tensor", "sA")
                        for pp in range(np_):
                            pair = pg2 * 2 + pp
                            for hf in range(2):
                                bg = PB[bank_rot[0] % 6]
                                bu_ = PB[(bank_rot[0] + 1) % 6]
                                bank_rot[0] += 2
                                wait("tensor", "ve")
                                for k in range(NKD):
                                    op("tensor", lambda e, k=k, pp=pp, cw=cw, bg=bg, hf=hf: e.matmul(
                                        bg[:], w1s[:, k * cw + pp * 256:k * cw + pp * 256 + 128],
                                        xnc3[k][:, hf * 512:(hf + 1) * 512],
                                        start=(k == 0), stop=(k == NKD - 1)))
                                    op("tensor", lambda e, k=k, pp=pp, cw=cw, bu_=bu_, hf=hf: e.matmul(
                                        bu_[:], w1s[:, k * cw + pp * 256 + 128:k * cw + pp * 256 + 256],
                                        xnc3[k][:, hf * 512:(hf + 1) * 512],
                                        start=(k == 0), stop=(k == NKD - 1)),
                                        inc=("pe" if k == NKD - 1 else None))
                                wait("scalar", "pe")
                                op("scalar", lambda e, bg=bg: e.activation(silt[:], bg[:], AF.Silu), inc="se")
                                wait("vector", "se")
                                op("vector", lambda e, bu_=bu_: e.tensor_mul(t2s[:], bu_[:], silt[:]))
                                op("vector", lambda e, pair=pair, hf=hf: e.tensor_mul(
                                    act[pair][:, hf * 512:(hf + 1) * 512], t2s[:], gbc[:, hf * 512:(hf + 1) * 512]), inc="ve")
                    for dmg in range(8):
                        wait("gpsimd", "pe")
                        for q in range(2):
                            rows = 768 if q == 0 else 640
                            nki = 6 if q == 0 else 5
                            op("gpsimd", lambda e, e_=e_, dmg=dmg, q=q, rows=rows, nki=nki: e.dma_start(
                                out=w2s[:, q * 6 * 256:q * 6 * 256 + nki * 256].rearrange("p (k c) -> p k c", k=nki),
                                in_=d_w2[e_, q * 768:q * 768 + rows, dmg * 256:(dmg + 1) * 256].rearrange(
                                    "(k p) c -> p k c", p=128)),
                                inc="sB", inc_n=16)
                        wait("tensor", "sB")
                        wait("tensor", "ve")
                        br = [0]
                        for dd in range(2):
                            dm = dmg * 2 + dd
                            for hf in range(2):
                                bk = PB[br[0] % 6]
                                br[0] += 1
                                for ki in range(NIP):
                                    op("tensor", lambda e, ki=ki, dd=dd, bk=bk, hf=hf: e.matmul(
                                        bk[:], w2s[:, ki * 256 + dd * 128:ki * 256 + dd * 128 + 128],
                                        act[ki][:, hf * 512:(hf + 1) * 512],
                                        start=(ki == 0), stop=(ki == NIP - 1)),
                                        inc=("pe" if ki == NIP - 1 else None))
                                wait("vector", "pe")
                                op("vector", lambda e, dm=dm, hf=hf, bk=bk: e.tensor_add(
                                    ffn[dm][:, hf * 512:(hf + 1) * 512],
                                    ffn[dm][:, hf * 512:(hf + 1) * 512], bk[:]), inc="ve")
                wait("sync", "ve")
                for dm in range(16):
                    op("sync", lambda e, dm=dm, t2=t2: e.dma_start(
                        out=d_out[dm * 128:(dm + 1) * 128, t2 * TT2:(t2 + 1) * TT2], in_=ffn[dm][:]),
                        inc="so", inc_n=16)
            wait("sync", "so")

        skip[0] = False
        if 3 not in phases:
            op("vector", lambda e: e.tensor_copy(dstg[:, 0:512], c_IDF32[:, 0:128].rearrange("p (a b) -> p (a b)", a=1) if False else c_IDF32[:]) if False else e.tensor_copy(c_TRIU[:], c_IDF32[:]), inc="ve")
            wait("sync", "ve")
            op("sync", lambda e: e.dma_start(out=d_out[0:128, 0:128], in_=c_TRIU[:]), inc="so", inc_n=16)
            wait("sync", "so")

        block = ctx.enter_context(nc.Block())

        @block.sync
        def _(eng):
            for f in streams["sync"]:
                f(eng)

        @block.gpsimd
        def _(eng):
            for f in streams["gpsimd"]:
                f(eng)

        @block.vector
        def _(eng):
            for f in streams["vector"]:
                f(eng)

        @block.scalar
        def _(eng):
            for f in streams["scalar"]:
                f(eng)

        @block.tensor
        def _(eng):
            for f in streams["tensor"]:
                f(eng)

    return nc


def host_layout(inputs):
    u = np.asarray(inputs["L0_attn_u"])
    W_in = np.asarray(inputs["L0_attn_W_in"])
    conv_w = np.asarray(inputs["L0_attn_conv_w"])
    conv_b = np.asarray(inputs["L0_attn_conv_b"])
    dt_bias = np.asarray(inputs["L0_attn_dt_bias"])
    A_log = np.asarray(inputs["L0_attn_A_log"])
    Dh = np.asarray(inputs["L0_attn_D"])
    nw = np.asarray(inputs["L0_attn_norm_weight"])
    W_out = np.asarray(inputs["L0_attn_W_out"])
    rw = np.asarray(inputs["L0_ffn_router_w"])
    w1 = np.asarray(inputs["L0_ffn_w1"])
    w2 = np.asarray(inputs["L0_ffn_w2"])

    WinT = np.ascontiguousarray(W_in.T)
    convW = np.ascontiguousarray(conv_w.reshape(34, 128, 4).transpose(1, 0, 2).reshape(128, 136))
    convB = np.ascontiguousarray(conv_b.reshape(34, 128).T)
    dtb = np.ascontiguousarray(dt_bias.reshape(64, 1))
    Alog = np.ascontiguousarray(A_log.reshape(64, 1))
    DCOL = np.ascontiguousarray(
        np.concatenate([np.broadcast_to(Dh[0::2], (64, 32)), np.broadcast_to(Dh[1::2], (64, 32))], axis=0))
    nwc = np.ascontiguousarray(nw.reshape(32, 128).T)
    WoutT = np.ascontiguousarray(W_out.T)
    rwT = np.ascontiguousarray(rw.T.reshape(16, 128, 8).transpose(1, 0, 2).reshape(128, 128))
    w1T = w1.transpose(0, 2, 1)
    w1p = np.empty_like(w1T)
    for j in range(NIP):
        w1p[:, :, j * 256:j * 256 + 128] = w1T[:, :, j * 128:(j + 1) * 128]
        w1p[:, :, j * 256 + 128:j * 256 + 256] = w1T[:, :, 1408 + j * 128:1408 + (j + 1) * 128]
    w1p = np.ascontiguousarray(w1p)
    w2p = np.ascontiguousarray(w2.transpose(0, 2, 1))

    consts = {
        "idbf": np.eye(128, dtype=bfnp),
        "idf32": np.eye(128, dtype=np.float32),
        "triu": np.ascontiguousarray(np.triu(np.ones((128, 128), dtype=np.float32))),
        "onesr": np.ones((1, 128), dtype=bfnp),
        "onesc": np.ones((128, 1), dtype=np.float32),
        "onescb": np.ones((128, 1), dtype=bfnp),
        "ones64": np.ones((64, 128), dtype=np.float32),
    }
    shared = {"WinT": WinT, "convW": convW, "convB": convB, "dtb": dtb,
              "Alog": Alog, "DCOL": DCOL, "nwc": nwc, "WoutT": WoutT,
              "rw": rwT, "w1p": w1p, "w2p": w2p}
    shared.update(consts)
    in_maps = []
    for c in range(8):
        m = dict(shared)
        m["uT"] = np.ascontiguousarray(u[c].T)
        in_maps.append(m)
    return in_maps


def kernel(**inputs):
    nc = build_nc()
    in_maps = host_layout(inputs)
    res = run_bass_kernel_spmd(nc, in_maps, core_ids=list(range(8)))
    out = np.stack([np.ascontiguousarray(np.asarray(r["outT"]).T) for r in res.results], axis=0)
    return out


if __name__ == "__main__":
    nc = build_nc()
    print("build ok")


# version 05
# speedup vs baseline: 1.1385x; 1.1385x over previous
import numpy as np
import ml_dtypes
from contextlib import ExitStack

import concourse.bass as bass
import concourse.mybir as mybir
from concourse.bass_utils import run_bass_kernel_spmd

F32 = mybir.dt.float32
BF16 = mybir.dt.bfloat16
AF = mybir.ActivationFunctionType
ALU = mybir.AluOpType
AX = mybir.AxisListType
bfnp = ml_dtypes.bfloat16

D = 2048
T = 2048
EI = 8512
DI = 4096
NH = 64
CH = 128
TT = 512
NTT = T // TT
CPT = TT // CH
NKD = D // 128
E = 8
I_FF = 1408
NIP = 11
TT2 = 1024
NTT2 = T // TT2
EPS = 1e-5

ENGINES = ("sync", "gpsimd", "vector", "scalar", "tensor")


def build_nc(race_detect=True, debug=False, phases=(0, 1, 2, 3), p1sub=99, p2sub=99):
    nc = bass.Bass("TRN2", target_bir_lowering=False, detect_race_conditions=race_detect)

    d_uT = nc.dram_tensor("uT", [D, T], F32, kind="ExternalInput")
    d_WinT = nc.dram_tensor("WinT", [D, EI], F32, kind="ExternalInput")
    d_convW = nc.dram_tensor("convW", [128, 136], F32, kind="ExternalInput")
    d_convB = nc.dram_tensor("convB", [128, 34], F32, kind="ExternalInput")
    d_dtb = nc.dram_tensor("dtb", [64, 1], F32, kind="ExternalInput")
    d_Alog = nc.dram_tensor("Alog", [64, 1], F32, kind="ExternalInput")
    d_DCOL = nc.dram_tensor("DCOL", [128, 32], F32, kind="ExternalInput")
    d_nw = nc.dram_tensor("nwc", [128, 32], F32, kind="ExternalInput")
    d_WoutT = nc.dram_tensor("WoutT", [DI, D], F32, kind="ExternalInput")
    d_rw = nc.dram_tensor("rw", [128, 128], F32, kind="ExternalInput")
    d_w1 = nc.dram_tensor("w1p", [E, D, 2816], F32, kind="ExternalInput")
    d_w2 = nc.dram_tensor("w2p", [E, I_FF, D], F32, kind="ExternalInput")
    d_IDBF = nc.dram_tensor("idbf", [128, 128], BF16, kind="ExternalInput")
    d_IDF32 = nc.dram_tensor("idf32", [128, 128], F32, kind="ExternalInput")
    d_TRIU = nc.dram_tensor("triu", [128, 128], F32, kind="ExternalInput")
    d_ONESR = nc.dram_tensor("onesr", [1, 128], BF16, kind="ExternalInput")
    d_ONESC = nc.dram_tensor("onesc", [128, 1], F32, kind="ExternalInput")
    d_ONESCB = nc.dram_tensor("onescb", [128, 1], BF16, kind="ExternalInput")
    d_ONES64 = nc.dram_tensor("ones64", [64, 128], F32, kind="ExternalInput")

    d_out = nc.dram_tensor("outT", [D, T], F32, kind="ExternalOutput")

    d_xn = nc.dram_tensor("xn_s", [D, T], BF16)
    d_g = nc.dram_tensor("g_s", [E, T], F32)
    d_attn = nc.dram_tensor("attn_s", [D, T], F32)
    d_acs = nc.dram_tensor("acs_s", [2, 64, 128], BF16)
    d_diag = nc.dram_tensor("diag_s", [34, 128, 512], BF16)
    d_silz = nc.dram_tensor("silz_s", [32, 128, TT], BF16)
    d_dbg = nc.dram_tensor("dbg_s", [16, 128, DI], BF16) if debug else None
    d_dbg2 = nc.dram_tensor("dbg2_s", [16, 128, DI], BF16) if debug else None
    d_dbg3 = nc.dram_tensor("dbg3_s", [16, 128, 256], F32) if debug else None
    d_dbg4 = nc.dram_tensor("dbg4_s", [16, 128, 128], F32) if debug else None
    d_dbg5 = nc.dram_tensor("dbg5_s", [16, 1, 16384], BF16) if debug else None
    d_dbg6 = nc.dram_tensor("dbg6_s", [16, 2, 128, 128], BF16) if debug else None
    d_dbg7 = nc.dram_tensor("dbg7_s", [16, 128, 128], F32) if debug else None

    streams = {e: [] for e in ENGINES}
    cnt = {}
    sems = {}

    skip = [False]

    def op(eng_name, f, inc=None, inc_n=1):
        if skip[0]:
            return
        if inc is not None:
            cnt[inc] += inc_n
            streams[eng_name].append(
                lambda eng, f=f, s=inc, n=inc_n: f(eng).then_inc(sems[s], n))
        else:
            streams[eng_name].append(lambda eng, f=f: f(eng))

    def wait(eng_name, sem_name, v=None):
        if skip[0]:
            return
        if v is None:
            v = cnt[sem_name]
        streams[eng_name].append(
            lambda eng, s=sem_name, v=v: eng.wait_ge(sems[s], v))

    SEM_NAMES = ["lc", "lu0", "lu1", "lx", "wA", "wB", "wC", "sA", "sB",
                 "ac0", "ac1", "sz0", "sz1", "gg", "ua0", "ua1", "so",
                 "pe", "ve", "se", "ge", "bar", "dgo", "mwr", "wk"]
    for s in SEM_NAMES:
        cnt[s] = 0

    def barrier():
        if skip[0]:
            return
        for e in ENGINES:
            streams[e].append(lambda eng: eng.nop().then_inc(sems["bar"], 1))
            cnt["bar"] += 1
        tgt = cnt["bar"]
        for e in ENGINES:
            wait(e, "bar", tgt)

    with ExitStack() as ctx:
        def sb(name, shape, dt):
            return ctx.enter_context(nc.sbuf_tensor(name, shape, dt))

        def psum(name, shape, dt):
            return ctx.enter_context(nc.psum_tensor(name, shape, dt))

        for s in SEM_NAMES:
            sems[s] = ctx.enter_context(nc.semaphore(s))

        PB = [psum(f"pb{i}", [128, 512], F32) for i in range(6)]
        PT = [psum(f"pt{i}", [128, 128], BF16) for i in range(2)]

        c_IDBF = sb("cIDBF", [128, 128], BF16)
        c_IDF32 = sb("cIDF32", [128, 128], F32)
        c_TRIU = sb("cTRIU", [128, 128], F32)
        c_ONESR = sb("cONESR", [1, 128], BF16)
        c_ONESC = sb("cONESC", [128, 1], F32)
        c_ONESCB = sb("cONESCB", [128, 1], BF16)
        c_rwh = sb("crwh", [128, 128], BF16)
        c_rwl = sb("crwl", [128, 128], BF16)
        c_rwstg = sb("crwstg", [128, 128], F32)
        c_ONES64 = sb("cONES64", [64, 128], F32)
        c_convW = sb("cconvW", [128, 136], F32)
        c_convB = sb("cconvB", [128, 34], F32)
        c_dtb = sb("cdtb", [64, 1], F32)
        c_Alog = sb("cAlog", [64, 1], F32)
        c_DCOL = sb("cDCOL", [128, 32], F32)
        c_nw = sb("cnw", [128, 32], F32)
        c_rw = sb("crw", [128, 128], F32)
        c_Acol = sb("cAcol", [64, 1], F32)
        c_cpool = sb("cpool", [128, 2], F32)
        nc.const_aps.register(F32, EPS, c_cpool[:, 0:1])
        nc.const_aps.register(F32, 1.0, c_cpool[:, 1:2])
        dstg = sb("dstg", [128, 1024], BF16)

        for (dst, src) in [(c_IDBF, d_IDBF), (c_IDF32, d_IDF32), (c_TRIU, d_TRIU),
                           (c_ONESR, d_ONESR), (c_ONESC, d_ONESC), (c_ONESCB, d_ONESCB),
                           (c_ONES64, d_ONES64),
                           (c_convW, d_convW), (c_convB, d_convB), (c_dtb, d_dtb),
                           (c_Alog, d_Alog), (c_DCOL, d_DCOL), (c_nw, d_nw), (c_rw, d_rw)]:
            op("sync", lambda e, a=dst, b=src: e.dma_start(out=a[:], in_=b[:]), inc="lc", inc_n=16)
        wait("scalar", "lc")
        op("scalar", lambda e: e.activation(c_rwh[:], c_rw[:], AF.Copy), inc="se")
        op("scalar", lambda e: e.activation(c_Acol[:], c_Alog[:], AF.Exp), inc="se")
        wait("vector", "se")
        op("vector", lambda e: e.tensor_copy(c_rwstg[:], c_rwh[:]), inc="ve")
        wait("gpsimd", "ve")
        op("gpsimd", lambda e: e.tensor_sub(c_rwl[:], c_rw[:], c_rwstg[:]), inc="ge")
        wait("vector", "lc")
        wait("vector", "se")
        op("vector", lambda e: e.tensor_scalar(c_Acol[:], c_Acol[:], -1.0, None, ALU.mult), inc="ve")
        op("vector", lambda e: e.memset(c_cpool[:, 0:1], EPS))
        op("vector", lambda e: e.memset(c_cpool[:, 1:2], 1.0))
        for rt in range(34):
            sl = rt % 2
            if rt >= 2:
                wait("vector", "dgo")
            for k in range(4):
                op("vector",
                   lambda e, rt=rt, k=k, sl=sl: e.tensor_scalar(
                       dstg[:, sl * 512 + k * 128:sl * 512 + (k + 1) * 128], c_IDBF[:],
                       c_convW[:, rt * 4 + k:rt * 4 + k + 1], None, ALU.mult),
                   inc=("ve" if k == 3 else None))
            wait("sync", "ve")
            op("sync", lambda e, rt=rt, sl=sl: e.dma_start(
                out=d_diag[rt], in_=dstg[:, sl * 512:(sl + 1) * 512]), inc="dgo", inc_n=16)
        wait("sync", "dgo")
        barrier()

        skip[0] = 1 not in phases
        p1_base_skip = [skip[0]]

        def p1s(n):
            skip[0] = p1_base_skip[0] or (n > p1sub)
        with ExitStack() as p1:
            def sb1(name, shape, dt):
                return p1.enter_context(nc.sbuf_tensor(name, shape, dt))
            u2 = [sb1(f"p1u{i}", [128, TT], F32) for i in range(2)]
            usq = [sb1(f"p1usq{i}", [128, TT], F32) for i in range(2)]
            ubf = [sb1(f"p1ub{i}", [128, TT], BF16) for i in range(2)]
            usqbf = [sb1(f"p1uqb{i}", [128, TT], BF16) for i in range(2)]
            xnf = [sb1(f"p1xnf{k}", [128, TT], F32) for k in range(NKD)]
            t1b = sb1("p1t1", [128, TT], F32)
            t2b = sb1("p1t2", [128, TT], F32)
            xnh = [sb1(f"p1xnh{k}", [128, TT], BF16) for k in range(NKD)]
            xnl = [sb1(f"p1xnl{k}", [128, TT], BF16) for k in range(NKD)]
            rowst = sb1("p1rows", [1, 6 * TT], F32)
            rowbf = sb1("p1rowbf", [1, 2 * TT], BF16)
            lg = sb1("p1lg", [128, 32], F32)
            m1c = sb1("p1m1", [128, 4], F32)
            m2c = sb1("p1m2", [128, 4], F32)
            msk1 = sb1("p1msk1", [128, 32], F32)
            msk2 = sb1("p1msk2", [128, 32], F32)
            mskd = sb1("p1mskd", [128, 32], F32)
            d1t = sb1("p1d1t", [128, 32], F32)
            sg1 = sb1("p1sg1", [128, 32], F32)
            t5t = sb1("p1t5t", [128, 32], F32)
            w1c = sb1("p1w1", [128, 4], F32)
            w2c = sb1("p1w2", [128, 4], F32)
            ecol = sb1("p1e", [128, 4], F32)
            ecx = sb1("p1ecx", [128, 4], F32)
            gtm = sb1("p1gtm", [128, 32], F32)
            gtm2 = sb1("p1gtm2", [128, 32], F32)
            gstage = sb1("p1gst", [32, 128], F32)

            P_SUM, P_SQ, P_MBC, P_RBC, P_LG, P_GT = PB[0], PB[1], PB[2], PB[3], PB[4], PB[5]
            R = rowst

            for tt in range(NTT):
                p1s(10)
                for k in range(NKD):
                    sl = u2[k % 2]
                    lus = "lu0" if k % 2 == 0 else "lu1"
                    if k >= 2:
                        wait("sync", "ve")
                    op("sync", lambda e, sl=sl, k=k, tt=tt: e.dma_start(
                        out=sl[:], in_=d_uT[k * 128:(k + 1) * 128, tt * TT:(tt + 1) * TT]),
                        inc=lus, inc_n=16)
                    wait("vector", lus)
                    if k >= 2:
                        wait("vector", "wk")
                    op("vector", lambda e, sl=sl, k=k: e.tensor_mul(usq[k % 2][:], sl[:], sl[:]))
                    op("vector", lambda e, sl=sl, k=k: e.tensor_copy(ubf[k % 2][:], sl[:]), inc="ve")
                    wait("gpsimd", "ve")
                    op("gpsimd", lambda e, k=k: e.tensor_copy(usqbf[k % 2][:], usq[k % 2][:]), inc="ge")
                    wait("tensor", "ge")
                    op("tensor", lambda e, k=k: e.matmul(
                        P_SUM[:1, :], c_ONESCB[:], ubf[k % 2][:], start=(k == 0), stop=(k == NKD - 1)))
                    op("tensor", lambda e, k=k: e.matmul(
                        P_SQ[:1, :], c_ONESCB[:], usqbf[k % 2][:], start=(k == 0), stop=(k == NKD - 1)),
                        inc=("pe" if k == NKD - 1 else "wk"))
                p1s(20)
                wait("vector", "pe")
                op("vector", lambda e: e.tensor_scalar(R[:, 0:TT], P_SUM[:1, :], 1.0 / D, None, ALU.mult))
                op("vector", lambda e: e.tensor_scalar(R[:, TT:2 * TT], P_SQ[:1, :], 1.0 / D, None, ALU.mult))
                op("vector", lambda e: e.tensor_mul(R[:, 5 * TT:6 * TT], R[:, 0:TT], R[:, 0:TT]))
                op("vector", lambda e: e.tensor_sub(R[:, 2 * TT:3 * TT], R[:, TT:2 * TT], R[:, 5 * TT:6 * TT]), inc="ve")
                p1s(21)
                wait("scalar", "ve")
                op("scalar", lambda e: e.activation(R[:, 3 * TT:4 * TT], R[:, 2 * TT:3 * TT], AF.Ln, bias=EPS), inc="se")
                p1s(22)
                wait("vector", "se")
                op("vector", lambda e: e.tensor_scalar(R[:, 5 * TT:6 * TT], R[:, 3 * TT:4 * TT], -0.5, None, ALU.mult), inc="ve")
                wait("scalar", "ve")
                op("scalar", lambda e: e.activation(R[:, 4 * TT:5 * TT], R[:, 5 * TT:6 * TT], AF.Exp), inc="se")
                p1s(23)
                wait("vector", "se")
                op("vector", lambda e: e.tensor_copy(rowbf[:, 0:TT], R[:, 0:TT]))
                op("vector", lambda e: e.tensor_copy(rowbf[:, TT:2 * TT], R[:, 4 * TT:5 * TT]), inc="ve")
                wait("tensor", "ve")
                op("tensor", lambda e: e.matmul(P_MBC[:], c_ONESR[:], rowbf[0:1, 0:TT], start=True, stop=True))
                op("tensor", lambda e: e.matmul(P_RBC[:], c_ONESR[:], rowbf[0:1, TT:2 * TT], start=True, stop=True), inc="pe")
                p1s(30)
                wait("sync", "pe")
                for k in range(NKD):
                    sl = u2[k % 2]
                    lus = "lu0" if k % 2 == 0 else "lu1"
                    if k >= 2:
                        wait("sync", "ve")
                    op("sync", lambda e, sl=sl, k=k, tt=tt: e.dma_start(
                        out=sl[:], in_=d_uT[k * 128:(k + 1) * 128, tt * TT:(tt + 1) * TT]),
                        inc=lus, inc_n=16)
                    wait("vector", lus)
                    op("vector", lambda e, sl=sl: e.tensor_sub(t1b[:], sl[:], P_MBC[:]))
                    op("vector", lambda e, k=k: e.tensor_mul(xnf[k][:], t1b[:], P_RBC[:]))
                    op("vector", lambda e, k=k: e.tensor_copy(xnh[k][:], xnf[k][:]), inc="ve")
                    wait("gpsimd", "ve")
                    op("gpsimd", lambda e, k=k: e.tensor_copy(t2b[:], xnh[k][:]), inc="ge")
                    wait("vector", "ge")
                    op("vector", lambda e, k=k: e.tensor_sub(xnl[k][:], xnf[k][:], t2b[:]), inc="ve")
                    wait("sync", "ve")
                    op("sync", lambda e, k=k, tt=tt: e.dma_start(
                        out=d_xn[k * 128:(k + 1) * 128, tt * TT:(tt + 1) * TT], in_=xnh[k][:]),
                        inc="mwr", inc_n=16)
                p1s(40)
                wait("tensor", "ve")
                for sub in range(TT // 128):
                    for k in range(NKD):
                        op("tensor", lambda e, sub=sub, k=k: e.matmul(
                            P_LG[:, sub * 8:(sub + 1) * 8], xnh[k][:, sub * 128:(sub + 1) * 128],
                            c_rwh[:, k * 8:(k + 1) * 8], start=(k == 0), stop=False))
                        op("tensor", lambda e, sub=sub, k=k: e.matmul(
                            P_LG[:, sub * 8:(sub + 1) * 8], xnh[k][:, sub * 128:(sub + 1) * 128],
                            c_rwl[:, k * 8:(k + 1) * 8], start=False, stop=False))
                        op("tensor", lambda e, sub=sub, k=k: e.matmul(
                            P_LG[:, sub * 8:(sub + 1) * 8], xnl[k][:, sub * 128:(sub + 1) * 128],
                            c_rwh[:, k * 8:(k + 1) * 8], start=False, stop=(k == NKD - 1)),
                            inc=("pe" if (k == NKD - 1 and sub == 3) else None))
                p1s(50)
                def bc4(t):
                    return bass.AP(t if not hasattr(t, "tensor") else t.tensor, 0, [[4, 128], [1, 4], [0, 8]])
                def seg(t):
                    return t[:].rearrange("p (s e) -> p s e", s=4)
                def col4(t):
                    return t[:].rearrange("p (s o) -> p s o", o=1)
                wait("scalar", "pe")
                op("scalar", lambda e: e.activation(lg[:], P_LG[:, 0:32], AF.Copy), inc="se")
                wait("vector", "se")
                op("vector", lambda e: e.reduce_max(col4(m1c), seg(lg), axis=AX.X), inc="ve")
                wait("gpsimd", "ve")
                op("gpsimd", lambda e: e.tensor_tensor(seg(d1t), seg(lg), bc4(m1c), ALU.subtract), inc="ge")
                wait("scalar", "ge")
                op("scalar", lambda e: e.activation(sg1[:], d1t[:], AF.Sign), inc="se")
                wait("gpsimd", "se")
                op("gpsimd", lambda e: e.tensor_scalar(msk1[:], sg1[:], 1.0, None, ALU.add), inc="ge")
                wait("vector", "ge")
                op("vector", lambda e: e.tensor_scalar(t5t[:], msk1[:], -1e30, None, ALU.mult), inc="ve")
                wait("gpsimd", "ve")
                op("gpsimd", lambda e: e.tensor_add(mskd[:], t5t[:], lg[:]), inc="ge")
                wait("vector", "ge")
                op("vector", lambda e: e.reduce_max(col4(m2c), seg(mskd), axis=AX.X), inc="ve")
                wait("gpsimd", "ve")
                op("gpsimd", lambda e: e.tensor_tensor(seg(d1t), seg(mskd), bc4(m2c), ALU.subtract), inc="ge")
                wait("scalar", "ge")
                op("scalar", lambda e: e.activation(sg1[:], d1t[:], AF.Sign), inc="se")
                wait("vector", "se")
                op("vector", lambda e: e.tensor_scalar(msk2[:], sg1[:], 1.0, None, ALU.add))
                op("vector", lambda e: e.tensor_sub(ecol[:], m2c[:], m1c[:]), inc="ve")
                wait("scalar", "ve")
                op("scalar", lambda e: e.activation(ecx[:], ecol[:], AF.Exp), inc="se")
                wait("gpsimd", "se")
                op("gpsimd", lambda e: e.tensor_scalar(w2c[:], ecx[:], 1.0, None, ALU.add), inc="ge")
                wait("vector", "ge")
                op("vector", lambda e: e.reciprocal(w1c[:], w2c[:]), inc="ve")
                wait("gpsimd", "ve")
                op("gpsimd", lambda e: e.tensor_mul(w2c[:], ecx[:], w1c[:]), inc="ge")
                wait("vector", "ge")
                op("vector", lambda e: e.tensor_tensor(seg(gtm2), seg(msk2), bc4(w2c), ALU.mult), inc="ve")
                wait("gpsimd", "ve")
                op("gpsimd", lambda e: e.tensor_tensor(seg(gtm), seg(msk1), bc4(w1c), ALU.mult), inc="ge")
                wait("vector", "ge")
                op("vector", lambda e: e.tensor_add(gtm[:], gtm[:], gtm2[:]), inc="ve")
                p1s(60)
                wait("tensor", "ve")
                op("tensor", lambda e: e.transpose(P_GT[0:32, 0:128], gtm[:], c_IDF32[:]), inc="pe")
                wait("vector", "pe")
                op("vector", lambda e: e.tensor_copy(gstage[:], P_GT[0:32, 0:128]), inc="ve")
                wait("sync", "ve")
                for sub in range(4):
                    op("sync", lambda e, tt=tt, sub=sub: e.dma_start(
                        out=d_g[:, tt * TT + sub * 128:tt * TT + (sub + 1) * 128],
                        in_=gstage[sub * 8:(sub + 1) * 8, :]), inc="gg", inc_n=16)
            p1s(90)
            skip[0] = p1_base_skip[0]
            wait("sync", "gg")
            wait("sync", "mwr")
            barrier()

        skip[0] = 2 not in phases
        p2_base_skip = [skip[0]]

        def p2s(n):
            skip[0] = p2_base_skip[0] or (n > p2sub)
        with ExitStack() as p2:
            def sb2(name, shape, dt):
                return p2.enter_context(nc.sbuf_tensor(name, shape, dt))
            xnc = [sb2(f"p2xn{k}", [128, TT], BF16) for k in range(NKD)]
            wp = [sb2(f"p2wp{i}", [128, 768], BF16) for i in range(3)]
            pre = [sb2(f"p2pre{i}", [128, 516], BF16) for i in range(6)]
            halo = sb2("p2halo", [128, 3 * 34], BF16)
            dgt = [sb2(f"p2dg{i}", [128, 512], BF16) for i in range(2)]
            cvo = [sb2(f"p2cv{i}", [128, TT], BF16) for i in range(34)]
            szst = [sb2(f"p2sz{i}", [128, 32 * 128], BF16) for i in range(2)]
            dtf = sb2("p2dtf", [64, TT], F32)
            sp1 = sb2("p2sp1", [64, TT], F32)
            dAf = sb2("p2dAf", [64, 128], F32)
            acsf = sb2("p2acs", [64, 128], F32)
            acshi = sb2("p2acshi", [64, 128], BF16)
            acshif = sb2("p2acshif", [64, 128], F32)
            acslo = sb2("p2acslo", [64, 128], BF16)
            apack = sb2("p2apack", [1, 16384], BF16)
            negat = sb2("p2negat", [128, 64], F32)
            dttm = sb2("p2dttm", [128, 64], F32)
            wexp = sb2("p2wexp", [128, 64], F32)
            alrow = sb2("p2alrow", [1, 64], F32)
            dkrow = sb2("p2dkrow", [1, 192], F32)
            dkbf = sb2("p2dkbf", [1, 128], BF16)
            dkbc = sb2("p2dkbc", [128, 64], F32)
            mscr = sb2("p2mscr", [128, 128], F32)
            d12 = sb2("p2d12", [128, 128], F32)
            expd = sb2("p2expd", [128, 128], F32)
            eabf = [sb2(f"p2ea{i}", [128, 128], BF16) for i in range(2)]
            mtil = [sb2(f"p2mt{i}", [128, 128], BF16) for i in range(2)]
            ctil = [sb2(f"p2ct{i}", [128, 128], BF16) for i in range(2)]
            xw = [sb2(f"p2xw{i}", [128, 64], BF16) for i in range(2)]
            xtm = sb2("p2xtm", [128, DI], BF16)
            btm = sb2("p2btm", [128, 128], BF16)
            S = sb2("p2S", [128, DI], F32)
            Sbf = sb2("p2Sbf", [128, DI], BF16)
            sstg = sb2("p2sstg", [128, 64], F32)
            yg = sb2("p2yg", [128, DI], BF16)
            ytmp = sb2("p2ytmp", [128, 128], F32)
            ytmp2 = sb2("p2ytmp2", [128, 128], F32)
            ygsq = [sb2(f"p2ygsq{i}", [128, 128], BF16) for i in range(2)]
            rrow = sb2("p2rrow", [1, 128], F32)
            rrow2 = sb2("p2rrow2", [1, 128], F32)
            rbfr = sb2("p2rbfr", [1, 128], BF16)
            ynorm = [sb2(f"p2yn{i}", [128, TT], BF16) for i in range(32)]
            astg = [sb2(f"p2as{i}", [128, 512], F32) for i in range(1)]

            for i8 in range(8):
                op("vector", lambda e, i8=i8: e.memset(S[:, i8 * 512:(i8 + 1) * 512], 0.0),
                   inc=("ve" if i8 == 7 else None))

            NE_TILE = 67
            pidx = [0]
            wslot = [0, 0, 0]

            def wsem(i):
                return ("wA", "wB", "wC")[i]

            def stream_piece(dram, k, c0, ncols):
                i = pidx[0]
                sl = i % 3
                if i >= 3:
                    wait("gpsimd", "wk", cnt["wk"] - 2)
                op("gpsimd", lambda e, sl=sl, k=k, c0=c0, ncols=ncols, dram=dram: e.dma_start(
                    out=wp[sl][:, 0:ncols], in_=dram[k * 128:(k + 1) * 128, c0:c0 + ncols]),
                    inc=wsem(sl), inc_n=16)
                wslot[sl] += 1
                wait("tensor", wsem(sl), 16 * wslot[sl])
                pidx[0] += 1
                return sl

            for tt in range(NTT):
                wait("sync", "wk")
                wait("sync", "pe")
                for k in range(NKD):
                    op("sync", lambda e, k=k, tt=tt: e.dma_start(
                        out=xnc[k][:], in_=d_xn[k * 128:(k + 1) * 128, tt * TT:(tt + 1) * TT]),
                        inc="lx", inc_n=16)
                wait("tensor", "lx")

                groups = []
                et = 0
                while et < NE_TILE:
                    groups.append(list(range(et, min(et + 6, NE_TILE))))
                    et += 6
                prslot = {}
                pre_ring = [0]
                conv_ctr = [0]
                for gi, g in enumerate(groups):
                    p2s(11)
                    wait("tensor", "ve")
                    wait("tensor", "se")
                    for k in range(NKD):
                        c0 = g[0] * 128
                        ncols = sum(64 if e_ == 66 else 128 for e_ in g)
                        sl = stream_piece(d_WinT, k, c0, ncols)
                        col = 0
                        for j, e_ in enumerate(g):
                            w_ = 64 if e_ == 66 else 128
                            tp = (0, 0)
                            op("tensor", lambda e, j=j, sl=sl, col=col, w_=w_, k=k, tp=tp: e.matmul(
                                PB[j][0:w_, :] if w_ == 64 else PB[j][:, :],
                                wp[sl][:, col:col + w_], xnc[k][:],
                                start=(k == 0), stop=(k == NKD - 1), tile_position=tp),
                                inc=("wk" if j == len(g) - 1 else None))
                            col += w_
                    p2s(12)
                    wait("vector", "wk")
                    wait("scalar", "wk")
                    for j, e_ in enumerate(g):
                        if e_ < 32:
                            ssl = e_ % 2
                            if e_ >= 2:
                                wait("scalar", "sz0" if ssl == 0 else "sz1")
                            op("scalar", lambda e, j=j, ssl=ssl: e.activation(
                                dstg[:, ssl * 512:(ssl + 1) * 512], PB[j][:], AF.Silu), inc="se")
                            wait("sync", "se")
                            op("sync", lambda e, e_=e_, ssl=ssl: e.dma_start(
                                out=d_silz[e_], in_=dstg[:, ssl * 512:(ssl + 1) * 512]),
                                inc=("sz0" if ssl == 0 else "sz1"), inc_n=16)
                        elif e_ < 66:
                            rt = e_ - 32
                            ps_ = pre_ring[0] % 6
                            prslot[rt] = ps_
                            pre_ring[0] += 1
                            pr = pre[ps_]
                            if tt == 0:
                                op("vector", lambda e, pr=pr: e.memset(pr[:, 0:3], 0.0))
                            else:
                                op("vector", lambda e, pr=pr, rt=rt: e.tensor_copy(
                                    pr[:, 0:3], halo[:, rt * 3:(rt + 1) * 3]))
                            op("vector", lambda e, pr=pr, j=j: e.tensor_copy(pr[:, 3:515], PB[j][:]))
                            op("vector", lambda e, j=j, rt=rt: e.tensor_copy(
                                halo[:, rt * 3:(rt + 1) * 3], PB[j][:, 509:512]), inc="ve")
                        else:
                            op("vector", lambda e, j=j: e.tensor_copy(dtf[0:64, :], PB[j][0:64, :]), inc="ve")
                    p2s(13)
                    for j, e_ in enumerate(g):
                        if not (32 <= e_ < 66):
                            continue
                        rt = e_ - 32
                        di_ = conv_ctr[0] % 2
                        conv_ctr[0] += 1
                        dcs = "ac0" if di_ == 0 else "ac1"
                        if conv_ctr[0] > 2:
                            wait("sync", "pe")
                        op("sync", lambda e, rt=rt, di_=di_: e.dma_start(
                            out=dgt[di_][:], in_=d_diag[rt]), inc=dcs, inc_n=16)
                        wait("tensor", dcs)
                        wait("tensor", "ve")
                        for kk in range(4):
                            op("tensor", lambda e, j=j, di_=di_, kk=kk, rt=rt: e.matmul(
                                PB[j][:], dgt[di_][:, kk * 128:(kk + 1) * 128],
                                pre[prslot[rt]][:, kk:kk + 512],
                                start=(kk == 0), stop=(kk == 3)),
                                inc=("pe" if kk == 3 else None))
                        wait("scalar", "pe")
                        op("scalar", lambda e, rt=rt, j=j: e.activation(
                            cvo[rt][:], PB[j][:], AF.Silu, bias=c_convB[:, rt:rt + 1]), inc="se")

                p2s(2)
                wait("vector", "ve")
                op("vector", lambda e: e.tensor_scalar(dtf[0:64, :], dtf[0:64, :], c_dtb[:], None, ALU.add), inc="ve")
                wait("scalar", "ve")
                op("scalar", lambda e: e.activation(sp1[:], dtf[0:64, :], AF.Abs))
                op("scalar", lambda e: e.activation(sp1[:], sp1[:], AF.Exp, scale=-1.0))
                op("scalar", lambda e: e.activation(sp1[:], sp1[:], AF.Ln, bias=1.0), inc="se")
                wait("vector", "se")
                op("vector", lambda e: e.tensor_scalar(dtf[0:64, :], dtf[0:64, :], 0.0, None, ALU.max))
                op("vector", lambda e: e.tensor_add(dtf[0:64, :], dtf[0:64, :], sp1[:]), inc="ve")

                p2s(3)
                for cc in range(CPT):
                    p2s(3)
                    col0 = cc * 128
                    ssl = szst[cc % 2]
                    wait("sync", "ve")
                    wait("sync", "sz0")
                    wait("sync", "sz1")
                    szs = "lu0" if cc % 2 == 0 else "lu1"
                    for q in range(4):
                        op("sync", lambda e, ssl=ssl, q=q, col0=col0: e.dma_start(
                            out=ssl[:, q * 8 * 128:(q + 1) * 8 * 128].rearrange("p (t c) -> p t c", t=8),
                            in_=d_silz[q * 8:(q + 1) * 8, :, col0:col0 + 128].rearrange("t p c -> p t c")),
                            inc=szs, inc_n=16)
                    wait("gpsimd", "ve")
                    op("gpsimd", lambda e, col0=col0: e.tensor_scalar(
                        dAf[:], dtf[0:64, col0:col0 + 128], c_Acol[:], None, ALU.mult), inc="ge")
                    wait("vector", "ge")
                    op("vector", lambda e: e.tensor_tensor_scan(
                        acsf[:], c_ONES64[:], dAf[:], 0.0, ALU.mult, ALU.add), inc="ve")
                    wait("scalar", "ve")
                    op("scalar", lambda e: e.activation(acshi[:], acsf[:], AF.Copy), inc="se")
                    wait("vector", "se")
                    op("vector", lambda e: e.tensor_copy(acshif[:], acshi[:]), inc="ve")
                    wait("gpsimd", "ve")
                    op("gpsimd", lambda e: e.tensor_sub(acslo[:], acsf[:], acshif[:]), inc="ge")
                    wait("vector", "ge")
                    wait("sync", "ve")
                    wait("sync", "se")
                    wait("sync", "ge")
                    wait("sync", "pe")
                    wait("sync", "ac1")
                    op("sync", lambda e: e.dma_start(out=d_acs[0], in_=acshi[:]), inc="ac0", inc_n=16)
                    op("sync", lambda e: e.dma_start(out=d_acs[1], in_=acslo[:]), inc="ac0", inc_n=16)
                    wait("sync", "ac0")
                    op("sync", lambda e: e.dma_start(
                        out=apack[:], in_=d_acs[:].rearrange("a h l -> (a h) l").rearrange("(o x) l -> o (x l)", o=1)),
                        inc="ac1", inc_n=16)
                    wait("tensor", "ve")
                    wait("tensor", "se")
                    op("tensor", lambda e: e.transpose(PB[0][0:128, 0:64], acsf[:], c_IDF32[0:64, 0:64]), inc="pe")
                    wait("vector", "pe")
                    op("vector", lambda e: e.tensor_scalar(negat[:], PB[0][:, 0:64], -1.0, None, ALU.mult), inc="ve")
                    wait("tensor", "ve")
                    op("tensor", lambda e, col0=col0: e.transpose(
                        PB[0][0:128, 64:128], dtf[0:64, col0:col0 + 128], c_IDF32[0:64, 0:64]), inc="pe")
                    wait("vector", "pe")
                    op("vector", lambda e: e.tensor_copy(dttm[:], PB[0][:, 64:128]), inc="ve")
                    wait("tensor", "se")
                    wait("tensor", "ve")
                    op("tensor", lambda e, col0=col0: e.transpose(
                        PT[0][:], cvo[32][:, col0:col0 + 128], c_IDBF[:]), inc="pe")
                    wait("vector", "pe")
                    op("vector", lambda e: e.tensor_copy(btm[:], PT[0][:]), inc="ve")
                    for xt_ in range(32):
                        wait("tensor", "ve")
                        op("tensor", lambda e, xt_=xt_, col0=col0: e.transpose(
                            PT[xt_ % 2][:], cvo[xt_][:, col0:col0 + 128], c_IDBF[:]), inc="pe")
                        wait("vector", "pe")
                        op("vector", lambda e, xt_=xt_: e.tensor_copy(
                            xtm[:, xt_ * 128:(xt_ + 1) * 128], PT[xt_ % 2][:]), inc="ve")
                    op("tensor", lambda e, col0=col0: e.matmul(
                        PB[1][:, 0:128], cvo[32][:, col0:col0 + 128], cvo[33][:, col0:col0 + 128],
                        start=True, stop=True), inc="pe")
                    wait("vector", "pe")
                    op("vector", lambda e: e.tensor_mul(mscr[:], PB[1][:, 0:128], c_TRIU[:]), inc="ve")
                    for i8 in range(8):
                        op("vector", lambda e, i8=i8: e.tensor_copy(
                            Sbf[:, i8 * 512:(i8 + 1) * 512], S[:, i8 * 512:(i8 + 1) * 512]),
                            inc=("ve" if i8 == 7 else None))
                    wait("vector", "ac1")
                    op("vector", lambda e: e.tensor_add(alrow[:], apack[0:1, 127:8192:128], apack[0:1, 8319:16384:128]))
                    op("vector", lambda e: e.tensor_copy(dkbf[0:1, 0:64], apack[0:1, 127:8192:128]))
                    op("vector", lambda e: e.tensor_copy(dkbf[0:1, 64:128], apack[0:1, 8319:16384:128]), inc="ve")
                    wait("tensor", "ve")
                    op("tensor", lambda e: e.matmul(PB[0][:, 128:192], c_ONESR[:], dkbf[0:1, 0:64], start=True, stop=False))
                    op("tensor", lambda e: e.matmul(PB[0][:, 128:192], c_ONESR[:], dkbf[0:1, 64:128], start=False, stop=True), inc="pe")
                    wait("vector", "pe")
                    op("vector", lambda e: e.tensor_add(wexp[:], PB[0][:, 128:192], negat[:]), inc="ve")
                    wait("scalar", "ve")
                    op("scalar", lambda e: e.activation(wexp[:], wexp[:], AF.Exp))
                    op("scalar", lambda e: e.activation(dkrow[0:1, 0:64], alrow[:], AF.Exp), inc="se")
                    wait("vector", "se")
                    op("vector", lambda e: e.tensor_mul(wexp[:], wexp[:], dttm[:]))
                    op("vector", lambda e: e.tensor_copy(dkbf[0:1, 0:64], dkrow[0:1, 0:64]), inc="ve")
                    wait("scalar", "ve")
                    op("scalar", lambda e: e.activation(dkrow[0:1, 64:128], dkbf[0:1, 0:64], AF.Copy), inc="se")
                    wait("gpsimd", "se")
                    op("gpsimd", lambda e: e.tensor_sub(dkrow[0:1, 128:192], dkrow[0:1, 0:64], dkrow[0:1, 64:128]), inc="ge")
                    wait("vector", "ge")
                    op("vector", lambda e: e.tensor_copy(dkbf[0:1, 64:128], dkrow[0:1, 128:192]), inc="ve")
                    wait("tensor", "ve")
                    op("tensor", lambda e: e.matmul(PB[0][:, 192:256], c_ONESR[:], dkbf[0:1, 0:64], start=True, stop=False))
                    op("tensor", lambda e: e.matmul(PB[0][:, 192:256], c_ONESR[:], dkbf[0:1, 64:128], start=False, stop=True), inc="pe")
                    wait("vector", "pe")
                    op("vector", lambda e: e.tensor_copy(dkbc[:], PB[0][:, 192:256]), inc="ve")

                    p2s(4)
                    wait("vector", "lu0" if cc % 2 == 0 else "lu1")
                    for hg in range(16):
                        wait("tensor", "se")
                        wait("tensor", "ve")
                        op("tensor", lambda e, hg=hg: e.matmul(
                            PB[3][:], c_ONESR[:], apack[0:1, hg * 512:(hg + 1) * 512], start=True, stop=False))
                        op("tensor", lambda e, hg=hg: e.matmul(
                            PB[3][:], c_ONESR[:], apack[0:1, 8192 + hg * 512:8192 + (hg + 1) * 512], start=False, stop=True), inc="pe")
                        wait("vector", "pe")
                        wait("scalar", "pe")
                        for hh in range(4):
                            h = hg * 4 + hh
                            par = h % 2
                            pair = h // 2
                            ybank = PB[2] if pair % 2 == 0 else PB[5]
                            op("vector", lambda e, hh=hh, h=h: e.tensor_scalar(
                                d12[:], PB[3][:, hh * 128:(hh + 1) * 128], negat[:, h:h + 1], 0.0, ALU.add, ALU.min), inc="ve")
                            wait("scalar", "ve")
                            op("scalar", lambda e: e.activation(expd[:], d12[:], AF.Exp))
                            op("scalar", lambda e, hh=hh, par=par: e.activation(
                                eabf[par][:], PB[3][:, hh * 128:(hh + 1) * 128], AF.Exp), inc="se")
                            wait("vector", "se")
                            op("vector", lambda e, h=h, par=par: e.scalar_tensor_tensor(
                                mtil[par][:], expd[:], dttm[:, h:h + 1], mscr[:], ALU.mult, ALU.mult))
                            op("vector", lambda e, par=par, col0=col0: e.tensor_mul(
                                ctil[par][:], cvo[33][:, col0:col0 + 128], eabf[par][:]))
                            op("vector", lambda e, h=h, par=par: e.tensor_scalar(
                                xw[par][:], xtm[:, h * 64:(h + 1) * 64], wexp[:, h:h + 1], None, ALU.mult), inc="ve")
                            if debug and h < 2:
                                wait("sync", "ve")
                                op("sync", lambda e, tt=tt, cc=cc, h=h, par=par: e.dma_start(
                                    out=d_dbg6[tt * CPT + cc, h], in_=mtil[par][:]), inc="dgo", inc_n=16)
                                wait("vector", "dgo")
                            wait("tensor", "ve")
                            tp = (0, 0) if par == 0 else (0, 64)
                            pr0, pr1 = (0, 64) if par == 0 else (64, 128)
                            if debug != "off":
                                op("tensor", lambda e, h=h, par=par, yb=ybank, tp=tp, pr0=pr0, pr1=pr1: e.matmul(
                                    yb[pr0:pr1, 0:128], xtm[:, h * 64:(h + 1) * 64], mtil[par][:],
                                    start=True, stop=(debug == "diag"), tile_position=tp),
                                    inc=("pe" if debug == "diag" else None))
                            if debug != "diag":
                                op("tensor", lambda e, h=h, par=par, yb=ybank, tp=tp, pr0=pr0, pr1=pr1: e.matmul(
                                    yb[pr0:pr1, 0:128], Sbf[:, h * 64:(h + 1) * 64], ctil[par][:],
                                    start=(debug == "off"), stop=True, tile_position=tp), inc="pe")
                            op("tensor", lambda e, h=h, par=par: e.matmul(
                                PB[4][:, (h % 8) * 64:(h % 8) * 64 + 64], btm[:], xw[par][:],
                                start=True, stop=True), inc="pe")
                            wait("scalar", "pe")
                            op("scalar", lambda e, h=h: e.activation(
                                sstg[:], S[:, h * 64:(h + 1) * 64], AF.Copy, scale=dkbc[:, h:h + 1]), inc="se")
                            wait("vector", "se")
                            wait("vector", "pe")
                            op("vector", lambda e, h=h: e.tensor_add(
                                S[:, h * 64:(h + 1) * 64], sstg[:], PB[4][:, (h % 8) * 64:(h % 8) * 64 + 64]), inc="ve")
                            if par == 1:
                                wait("scalar", "ve")
                                op("scalar", lambda e, pair=pair, col0=col0: e.activation(
                                    ytmp[:], cvo[pair][:, col0:col0 + 128], AF.Copy,
                                    scale=c_DCOL[:, pair:pair + 1]), inc="se")
                                wait("vector", "se")
                                op("vector", lambda e, yb=ybank: e.tensor_add(ytmp2[:], ytmp[:], yb[:, 0:128]), inc="ve")
                                wait("gpsimd", "ve")
                                op("gpsimd", lambda e, ssl=ssl, pair=pair: e.tensor_mul(
                                    yg[:, pair * 128:(pair + 1) * 128], ytmp2[:], ssl[:, pair * 128:(pair + 1) * 128]), inc="ge")
                                if debug and pair == 0:
                                    op("vector", lambda e, yb=ybank: e.tensor_copy(ytmp[:], yb[:, 0:128]), inc="ve")
                                    wait("sync", "ve")
                                    op("sync", lambda e, tt=tt, cc=cc: e.dma_start(
                                        out=d_dbg7[tt * CPT + cc], in_=ytmp[:]), inc="dgo", inc_n=16)
                                    wait("vector", "dgo")
                    if debug:
                        wait("sync", "ge")
                        wait("sync", "ve")
                        ci = tt * CPT + cc
                        op("sync", lambda e, ci=ci: e.dma_start(out=d_dbg[ci], in_=yg[:]), inc="dgo", inc_n=16)
                        op("sync", lambda e, ci=ci: e.dma_start(out=d_dbg2[ci], in_=xtm[:]), inc="dgo", inc_n=16)
                        op("sync", lambda e, ci=ci: e.dma_start(out=d_dbg3[ci][:, 0:64], in_=negat[:]), inc="dgo", inc_n=16)
                        op("sync", lambda e, ci=ci: e.dma_start(out=d_dbg3[ci][:, 64:128], in_=dttm[:]), inc="dgo", inc_n=16)
                        op("sync", lambda e, ci=ci: e.dma_start(out=d_dbg3[ci][:, 128:192], in_=wexp[:]), inc="dgo", inc_n=16)
                        op("sync", lambda e, ci=ci: e.dma_start(out=d_dbg3[ci][:, 192:256], in_=dkbc[:]), inc="dgo", inc_n=16)
                        op("sync", lambda e, ci=ci: e.dma_start(out=d_dbg4[ci], in_=mscr[:]), inc="dgo", inc_n=16)
                        op("sync", lambda e, ci=ci: e.dma_start(out=d_dbg5[ci], in_=apack[:]), inc="dgo", inc_n=16)
                        wait("vector", "dgo")
                        wait("tensor", "dgo")
                    p2s(5)
                    wait("vector", "ge")
                    for t32 in range(32):
                        if t32 >= 2:
                            wait("vector", "wk")
                        op("vector", lambda e, t32=t32: e.tensor_mul(
                            ygsq[t32 % 2][:], yg[:, t32 * 128:(t32 + 1) * 128], yg[:, t32 * 128:(t32 + 1) * 128]), inc="ve")
                        wait("tensor", "ve")
                        op("tensor", lambda e, t32=t32: e.matmul(
                            PB[0][0:1, 256:384], c_ONESCB[:], ygsq[t32 % 2][:],
                            start=(t32 == 0), stop=(t32 == 31)), inc=("pe" if t32 == 31 else "wk"))
                    wait("scalar", "pe")
                    op("scalar", lambda e: e.activation(rrow[:], PB[0][0:1, 256:384], AF.Ln, scale=1.0 / DI, bias=EPS), inc="se")
                    wait("vector", "se")
                    op("vector", lambda e: e.tensor_scalar(rrow2[:], rrow[:], -0.5, None, ALU.mult), inc="ve")
                    wait("scalar", "ve")
                    op("scalar", lambda e: e.activation(rrow[:], rrow2[:], AF.Exp), inc="se")
                    wait("gpsimd", "se")
                    op("gpsimd", lambda e: e.tensor_copy(rbfr[:], rrow[:]), inc="ge")
                    wait("tensor", "ge")
                    op("tensor", lambda e: e.matmul(PB[1][:, 128:256], c_ONESR[:], rbfr[:], start=True, stop=True), inc="pe")
                    wait("vector", "pe")
                    for t32 in range(32):
                        op("vector", lambda e, t32=t32, col0=col0: e.scalar_tensor_tensor(
                            ynorm[t32][:, col0:col0 + 128], yg[:, t32 * 128:(t32 + 1) * 128],
                            c_nw[:, t32:t32 + 1], PB[1][:, 128:256], ALU.mult, ALU.mult),
                            inc=("ve" if t32 == 31 else None))

                p2s(6)
                wgrp = [list(range(0, 6)), list(range(6, 12)), list(range(12, 16))]
                for gi, g in enumerate(wgrp):
                    c0 = g[0] * 128
                    ncols = len(g) * 128
                    wait("tensor", "ve")
                    wait("tensor", "se")
                    for k in range(32):
                        sl = stream_piece(d_WoutT, k, c0, ncols)
                        for j, dm in enumerate(g):
                            op("tensor", lambda e, j=j, sl=sl, k=k: e.matmul(
                                PB[j][:], wp[sl][:, j * 128:(j + 1) * 128], ynorm[k][:],
                                start=(k == 0), stop=(k == 31)),
                                inc=("wk" if j == len(g) - 1 else None))
                    wait("vector", "wk")
                    for j, dm in enumerate(g):
                        asl = astg[0]
                        uasem = "ua0"
                        wait("vector", uasem)
                        op("vector", lambda e, j=j, asl=asl: e.tensor_copy(asl[:], PB[j][:]), inc="ve")
                        wait("sync", "ve")
                        op("sync", lambda e, dm=dm, tt=tt, asl=asl: e.dma_start(
                            out=d_attn[dm * 128:(dm + 1) * 128, tt * TT:(tt + 1) * TT], in_=asl[:]),
                            inc=uasem, inc_n=16)
            p2s(99)
            skip[0] = p2_base_skip[0]
            wait("sync", "ua0")
            wait("sync", "ua1")
            barrier()

        skip[0] = 3 not in phases
        with ExitStack() as p3:
            def sb3(name, shape, dt):
                return p3.enter_context(nc.sbuf_tensor(name, shape, dt))
            xnc3 = [sb3(f"p3xn{k}", [128, TT2], BF16) for k in range(NKD)]
            ffn = [sb3(f"p3f{i}", [128, TT2], F32) for i in range(16)]
            act = [sb3(f"p3a{i}", [128, TT2], BF16) for i in range(NIP)]
            w1s2 = [sb3(f"p3w1s{i}", [128, 16 * 512], BF16) for i in range(2)]
            w2s2 = [sb3(f"p3w2s{i}", [128, 11 * 256], BF16) for i in range(2)]
            w1guard = [0, 0]
            w2guard = [0, 0]
            w1ctr = [0]
            w2ctr = [0]
            gst = sb3("p3gst", [1, TT2], F32)
            gbf = sb3("p3gbf", [1, TT2], BF16)
            gbc = sb3("p3gbc", [128, TT2], F32)
            silt = sb3("p3silt", [128, 512], F32)
            t2s = sb3("p3t2", [128, 512], F32)
            ustg = [sb3(f"p3u{i}", [128, TT2], F32) for i in range(2)]
            astg3 = [sb3(f"p3at{i}", [128, TT2], F32) for i in range(2)]

            for t2 in range(NTT2):
                wait("sync", "pe")
                for k in range(NKD):
                    op("sync", lambda e, k=k, t2=t2: e.dma_start(
                        out=xnc3[k][:], in_=d_xn[k * 128:(k + 1) * 128, t2 * TT2:(t2 + 1) * TT2]),
                        inc="lx", inc_n=16)
                for dm in range(16):
                    us, as_ = ustg[dm % 2], astg3[dm % 2]
                    uasem = "ua0" if dm % 2 == 0 else "ua1"
                    if dm >= 2:
                        wait("sync", "ve")
                    op("sync", lambda e, dm=dm, t2=t2, us=us: e.dma_start(
                        out=us[:], in_=d_uT[dm * 128:(dm + 1) * 128, t2 * TT2:(t2 + 1) * TT2]),
                        inc=uasem, inc_n=16)
                    op("sync", lambda e, dm=dm, t2=t2, as_=as_: e.dma_start(
                        out=as_[:], in_=d_attn[dm * 128:(dm + 1) * 128, t2 * TT2:(t2 + 1) * TT2]),
                        inc=uasem, inc_n=16)
                    wait("vector", uasem)
                    op("vector", lambda e, dm=dm, us=us, as_=as_: e.tensor_add(ffn[dm][:], us[:], as_[:]), inc="ve")
                wait("tensor", "lx")
                for e_ in range(E):
                    if not (t2 == 0 and e_ == 0):
                        wait("sync", "ve")
                    op("sync", lambda e, e_=e_, t2=t2: e.dma_start(
                        out=gst[:], in_=d_g[e_:e_ + 1, t2 * TT2:(t2 + 1) * TT2]), inc="gg", inc_n=16)
                    wait("vector", "gg")
                    op("vector", lambda e: e.tensor_copy(gbf[:], gst[:]), inc="ve")
                    wait("tensor", "ve")
                    wait("tensor", "pe", cnt["pe"])
                    op("tensor", lambda e: e.matmul(PB[0][:], c_ONESR[:], gbf[0:1, 0:512], start=True, stop=True))
                    op("tensor", lambda e: e.matmul(PB[1][:], c_ONESR[:], gbf[0:1, 512:1024], start=True, stop=True), inc="pe")
                    wait("vector", "pe")
                    op("vector", lambda e: e.tensor_copy(gbc[:, 0:512], PB[0][:]))
                    op("vector", lambda e: e.tensor_copy(gbc[:, 512:1024], PB[1][:]), inc="ve")
                    bank_rot = [0]
                    for pg2 in range(6):
                        np_ = 2 if pg2 < 5 else 1
                        cw = np_ * 256
                        wsl = w1ctr[0] % 2
                        w1ctr[0] += 1
                        w1s = w1s2[wsl]
                        wait("gpsimd", "pe", w1guard[wsl])
                        for q in range(4):
                            op("gpsimd", lambda e, e_=e_, pg2=pg2, q=q, cw=cw, w1s=w1s: e.dma_start(
                                out=w1s[:, q * 4 * cw:(q + 1) * 4 * cw].rearrange("p (k c) -> p k c", k=4),
                                in_=d_w1[e_, q * 512:(q + 1) * 512, pg2 * 512:pg2 * 512 + cw].rearrange(
                                    "(k p) c -> p k c", p=128)),
                                inc="sA", inc_n=16)
                        wait("tensor", "sA")
                        for pp in range(np_):
                            pair = pg2 * 2 + pp
                            for hf in range(2):
                                bg = PB[bank_rot[0] % 6]
                                bu_ = PB[(bank_rot[0] + 1) % 6]
                                bank_rot[0] += 2
                                wait("tensor", "ve")
                                for k in range(NKD):
                                    op("tensor", lambda e, k=k, pp=pp, cw=cw, bg=bg, hf=hf, w1s=w1s: e.matmul(
                                        bg[:], w1s[:, k * cw + pp * 256:k * cw + pp * 256 + 128],
                                        xnc3[k][:, hf * 512:(hf + 1) * 512],
                                        start=(k == 0), stop=(k == NKD - 1)))
                                    op("tensor", lambda e, k=k, pp=pp, cw=cw, bu_=bu_, hf=hf, w1s=w1s: e.matmul(
                                        bu_[:], w1s[:, k * cw + pp * 256 + 128:k * cw + pp * 256 + 256],
                                        xnc3[k][:, hf * 512:(hf + 1) * 512],
                                        start=(k == 0), stop=(k == NKD - 1)),
                                        inc=("pe" if k == NKD - 1 else None))
                                wait("scalar", "pe")
                                op("scalar", lambda e, bg=bg: e.activation(silt[:], bg[:], AF.Silu), inc="se")
                                wait("vector", "se")
                                op("vector", lambda e, bu_=bu_: e.tensor_mul(t2s[:], bu_[:], silt[:]))
                                op("vector", lambda e, pair=pair, hf=hf: e.tensor_mul(
                                    act[pair][:, hf * 512:(hf + 1) * 512], t2s[:], gbc[:, hf * 512:(hf + 1) * 512]), inc="ve")
                        w1guard[wsl] = cnt["pe"]
                    for dmg in range(8):
                        wsl2 = w2ctr[0] % 2
                        w2ctr[0] += 1
                        w2s = w2s2[wsl2]
                        wait("gpsimd", "pe", w2guard[wsl2])
                        for q in range(2):
                            rows = 768 if q == 0 else 640
                            nki = 6 if q == 0 else 5
                            op("gpsimd", lambda e, e_=e_, dmg=dmg, q=q, rows=rows, nki=nki, w2s=w2s: e.dma_start(
                                out=w2s[:, q * 6 * 256:q * 6 * 256 + nki * 256].rearrange("p (k c) -> p k c", k=nki),
                                in_=d_w2[e_, q * 768:q * 768 + rows, dmg * 256:(dmg + 1) * 256].rearrange(
                                    "(k p) c -> p k c", p=128)),
                                inc="sB", inc_n=16)
                        wait("tensor", "sB")
                        wait("tensor", "ve")
                        br = [0]
                        for dd in range(2):
                            dm = dmg * 2 + dd
                            for hf in range(2):
                                bk = PB[br[0] % 6]
                                br[0] += 1
                                for ki in range(NIP):
                                    op("tensor", lambda e, ki=ki, dd=dd, bk=bk, hf=hf, w2s=w2s: e.matmul(
                                        bk[:], w2s[:, ki * 256 + dd * 128:ki * 256 + dd * 128 + 128],
                                        act[ki][:, hf * 512:(hf + 1) * 512],
                                        start=(ki == 0), stop=(ki == NIP - 1)),
                                        inc=("pe" if ki == NIP - 1 else None))
                                wait("vector", "pe")
                                op("vector", lambda e, dm=dm, hf=hf, bk=bk: e.tensor_add(
                                    ffn[dm][:, hf * 512:(hf + 1) * 512],
                                    ffn[dm][:, hf * 512:(hf + 1) * 512], bk[:]), inc="ve")
                        w2guard[wsl2] = cnt["pe"]
                wait("sync", "ve")
                for dm in range(16):
                    op("sync", lambda e, dm=dm, t2=t2: e.dma_start(
                        out=d_out[dm * 128:(dm + 1) * 128, t2 * TT2:(t2 + 1) * TT2], in_=ffn[dm][:]),
                        inc="so", inc_n=16)
            wait("sync", "so")

        skip[0] = False
        if 3 not in phases:
            op("vector", lambda e: e.tensor_copy(dstg[:, 0:512], c_IDF32[:, 0:128].rearrange("p (a b) -> p (a b)", a=1) if False else c_IDF32[:]) if False else e.tensor_copy(c_TRIU[:], c_IDF32[:]), inc="ve")
            wait("sync", "ve")
            op("sync", lambda e: e.dma_start(out=d_out[0:128, 0:128], in_=c_TRIU[:]), inc="so", inc_n=16)
            wait("sync", "so")

        block = ctx.enter_context(nc.Block())

        @block.sync
        def _(eng):
            for f in streams["sync"]:
                f(eng)

        @block.gpsimd
        def _(eng):
            for f in streams["gpsimd"]:
                f(eng)

        @block.vector
        def _(eng):
            for f in streams["vector"]:
                f(eng)

        @block.scalar
        def _(eng):
            for f in streams["scalar"]:
                f(eng)

        @block.tensor
        def _(eng):
            for f in streams["tensor"]:
                f(eng)

    return nc


def host_layout(inputs):
    u = np.asarray(inputs["L0_attn_u"])
    W_in = np.asarray(inputs["L0_attn_W_in"])
    conv_w = np.asarray(inputs["L0_attn_conv_w"])
    conv_b = np.asarray(inputs["L0_attn_conv_b"])
    dt_bias = np.asarray(inputs["L0_attn_dt_bias"])
    A_log = np.asarray(inputs["L0_attn_A_log"])
    Dh = np.asarray(inputs["L0_attn_D"])
    nw = np.asarray(inputs["L0_attn_norm_weight"])
    W_out = np.asarray(inputs["L0_attn_W_out"])
    rw = np.asarray(inputs["L0_ffn_router_w"])
    w1 = np.asarray(inputs["L0_ffn_w1"])
    w2 = np.asarray(inputs["L0_ffn_w2"])

    WinT = np.ascontiguousarray(W_in.T)
    convW = np.ascontiguousarray(conv_w.reshape(34, 128, 4).transpose(1, 0, 2).reshape(128, 136))
    convB = np.ascontiguousarray(conv_b.reshape(34, 128).T)
    dtb = np.ascontiguousarray(dt_bias.reshape(64, 1))
    Alog = np.ascontiguousarray(A_log.reshape(64, 1))
    DCOL = np.ascontiguousarray(
        np.concatenate([np.broadcast_to(Dh[0::2], (64, 32)), np.broadcast_to(Dh[1::2], (64, 32))], axis=0))
    nwc = np.ascontiguousarray(nw.reshape(32, 128).T)
    WoutT = np.ascontiguousarray(W_out.T)
    rwT = np.ascontiguousarray(rw.T.reshape(16, 128, 8).transpose(1, 0, 2).reshape(128, 128))
    w1T = w1.transpose(0, 2, 1)
    w1p = np.empty_like(w1T)
    for j in range(NIP):
        w1p[:, :, j * 256:j * 256 + 128] = w1T[:, :, j * 128:(j + 1) * 128]
        w1p[:, :, j * 256 + 128:j * 256 + 256] = w1T[:, :, 1408 + j * 128:1408 + (j + 1) * 128]
    w1p = np.ascontiguousarray(w1p)
    w2p = np.ascontiguousarray(w2.transpose(0, 2, 1))

    consts = {
        "idbf": np.eye(128, dtype=bfnp),
        "idf32": np.eye(128, dtype=np.float32),
        "triu": np.ascontiguousarray(np.triu(np.ones((128, 128), dtype=np.float32))),
        "onesr": np.ones((1, 128), dtype=bfnp),
        "onesc": np.ones((128, 1), dtype=np.float32),
        "onescb": np.ones((128, 1), dtype=bfnp),
        "ones64": np.ones((64, 128), dtype=np.float32),
    }
    shared = {"WinT": WinT, "convW": convW, "convB": convB, "dtb": dtb,
              "Alog": Alog, "DCOL": DCOL, "nwc": nwc, "WoutT": WoutT,
              "rw": rwT, "w1p": w1p, "w2p": w2p}
    shared.update(consts)
    in_maps = []
    for c in range(8):
        m = dict(shared)
        m["uT"] = np.ascontiguousarray(u[c].T)
        in_maps.append(m)
    return in_maps


def kernel(**inputs):
    nc = build_nc()
    in_maps = host_layout(inputs)
    res = run_bass_kernel_spmd(nc, in_maps, core_ids=list(range(8)))
    out = np.stack([np.ascontiguousarray(np.asarray(r["outT"]).T) for r in res.results], axis=0)
    return out


if __name__ == "__main__":
    nc = build_nc()
    print("build ok")


# version 06 (final)
# speedup vs baseline: 1.2442x; 1.0928x over previous
import numpy as np
import ml_dtypes
from contextlib import ExitStack

import concourse.bass as bass
import concourse.mybir as mybir
from concourse.bass_utils import run_bass_kernel_spmd

F32 = mybir.dt.float32
BF16 = mybir.dt.bfloat16
AF = mybir.ActivationFunctionType
ALU = mybir.AluOpType
AX = mybir.AxisListType
bfnp = ml_dtypes.bfloat16

D = 2048
T = 2048
EI = 8512
DI = 4096
NH = 64
CH = 128
TT = 512
NTT = T // TT
CPT = TT // CH
NKD = D // 128
E = 8
I_FF = 1408
NIP = 11
TT2 = 1024
NTT2 = T // TT2
EPS = 1e-5

ENGINES = ("sync", "gpsimd", "vector", "scalar", "tensor")


def build_nc(race_detect=True, debug=False, phases=(0, 1, 2, 3), p1sub=99, p2sub=99):
    nc = bass.Bass("TRN2", target_bir_lowering=False, detect_race_conditions=race_detect)

    d_uT = nc.dram_tensor("uT", [D, T], F32, kind="ExternalInput")
    d_WinT = nc.dram_tensor("WinT", [D, EI], F32, kind="ExternalInput")
    d_convW = nc.dram_tensor("convW", [128, 136], F32, kind="ExternalInput")
    d_convB = nc.dram_tensor("convB", [128, 34], F32, kind="ExternalInput")
    d_dtb = nc.dram_tensor("dtb", [64, 1], F32, kind="ExternalInput")
    d_Alog = nc.dram_tensor("Alog", [64, 1], F32, kind="ExternalInput")
    d_DCOL = nc.dram_tensor("DCOL", [128, 32], F32, kind="ExternalInput")
    d_nw = nc.dram_tensor("nwc", [128, 32], F32, kind="ExternalInput")
    d_WoutT = nc.dram_tensor("WoutT", [DI, D], F32, kind="ExternalInput")
    d_rw = nc.dram_tensor("rw", [128, 128], F32, kind="ExternalInput")
    d_w1 = nc.dram_tensor("w1p", [E, D, 2816], F32, kind="ExternalInput")
    d_w2 = nc.dram_tensor("w2p", [E, I_FF, D], F32, kind="ExternalInput")
    d_IDBF = nc.dram_tensor("idbf", [128, 128], BF16, kind="ExternalInput")
    d_IDF32 = nc.dram_tensor("idf32", [128, 128], F32, kind="ExternalInput")
    d_TRIU = nc.dram_tensor("triu", [128, 128], F32, kind="ExternalInput")
    d_ONESR = nc.dram_tensor("onesr", [1, 128], BF16, kind="ExternalInput")
    d_ONESC = nc.dram_tensor("onesc", [128, 1], F32, kind="ExternalInput")
    d_ONESCB = nc.dram_tensor("onescb", [128, 1], BF16, kind="ExternalInput")
    d_ONES64 = nc.dram_tensor("ones64", [64, 128], F32, kind="ExternalInput")

    d_out = nc.dram_tensor("outT", [D, T], F32, kind="ExternalOutput")

    d_xn = nc.dram_tensor("xn_s", [D, T], BF16)
    d_g = nc.dram_tensor("g_s", [E, T], F32)
    d_attn = nc.dram_tensor("attn_s", [D, T], F32)
    d_acs = nc.dram_tensor("acs_s", [2, 64, 128], BF16)
    d_diag = nc.dram_tensor("diag_s", [34, 128, 512], BF16)
    d_silz = nc.dram_tensor("silz_s", [32, 128, TT], BF16)
    d_dbg = nc.dram_tensor("dbg_s", [16, 128, DI], BF16) if debug else None
    d_dbg2 = nc.dram_tensor("dbg2_s", [16, 128, DI], BF16) if debug else None
    d_dbg3 = nc.dram_tensor("dbg3_s", [16, 128, 256], F32) if debug else None
    d_dbg4 = nc.dram_tensor("dbg4_s", [16, 128, 128], F32) if debug else None
    d_dbg5 = nc.dram_tensor("dbg5_s", [16, 1, 16384], BF16) if debug else None
    d_dbg6 = nc.dram_tensor("dbg6_s", [16, 2, 128, 128], BF16) if debug else None
    d_dbg7 = nc.dram_tensor("dbg7_s", [16, 128, 128], F32) if debug else None

    streams = {e: [] for e in ENGINES}
    cnt = {}
    sems = {}

    skip = [False]

    def op(eng_name, f, inc=None, inc_n=1):
        if skip[0]:
            return
        if inc is not None:
            cnt[inc] += inc_n
            streams[eng_name].append(
                lambda eng, f=f, s=inc, n=inc_n: f(eng).then_inc(sems[s], n))
        else:
            streams[eng_name].append(lambda eng, f=f: f(eng))

    def wait(eng_name, sem_name, v=None):
        if skip[0]:
            return
        if v is None:
            v = cnt[sem_name]
        streams[eng_name].append(
            lambda eng, s=sem_name, v=v: eng.wait_ge(sems[s], v))

    SEM_NAMES = ["lc", "lu0", "lu1", "lx", "wA", "wB", "wC", "sA", "sB",
                 "ac0", "ac1", "sz0", "sz1", "gg", "ua0", "ua1", "so",
                 "pe", "ve", "se", "ge", "bar", "dgo", "mwr", "wk"]
    for s in SEM_NAMES:
        cnt[s] = 0

    def barrier():
        if skip[0]:
            return
        for e in ENGINES:
            streams[e].append(lambda eng: eng.nop().then_inc(sems["bar"], 1))
            cnt["bar"] += 1
        tgt = cnt["bar"]
        for e in ENGINES:
            wait(e, "bar", tgt)

    with ExitStack() as ctx:
        def sb(name, shape, dt):
            return ctx.enter_context(nc.sbuf_tensor(name, shape, dt))

        def psum(name, shape, dt):
            return ctx.enter_context(nc.psum_tensor(name, shape, dt))

        for s in SEM_NAMES:
            sems[s] = ctx.enter_context(nc.semaphore(s))

        PB = [psum(f"pb{i}", [128, 512], F32) for i in range(6)]
        PT = [psum(f"pt{i}", [128, 128], BF16) for i in range(2)]

        c_IDBF = sb("cIDBF", [128, 128], BF16)
        c_IDF32 = sb("cIDF32", [128, 128], F32)
        c_TRIU = sb("cTRIU", [128, 128], F32)
        c_ONESR = sb("cONESR", [1, 128], BF16)
        c_ONESC = sb("cONESC", [128, 1], F32)
        c_ONESCB = sb("cONESCB", [128, 1], BF16)
        c_rwh = sb("crwh", [128, 128], BF16)
        c_rwl = sb("crwl", [128, 128], BF16)
        c_rwstg = sb("crwstg", [128, 128], F32)
        c_ONES64 = sb("cONES64", [64, 128], F32)
        c_convW = sb("cconvW", [128, 136], F32)
        c_convB = sb("cconvB", [128, 34], F32)
        c_dtb = sb("cdtb", [64, 1], F32)
        c_Alog = sb("cAlog", [64, 1], F32)
        c_DCOL = sb("cDCOL", [128, 32], F32)
        c_nw = sb("cnw", [128, 32], F32)
        c_rw = sb("crw", [128, 128], F32)
        c_Acol = sb("cAcol", [64, 1], F32)
        c_cpool = sb("cpool", [128, 2], F32)
        nc.const_aps.register(F32, EPS, c_cpool[:, 0:1])
        nc.const_aps.register(F32, 1.0, c_cpool[:, 1:2])
        dstg = sb("dstg", [128, 1024], BF16)

        for (dst, src) in [(c_IDBF, d_IDBF), (c_IDF32, d_IDF32), (c_TRIU, d_TRIU),
                           (c_ONESR, d_ONESR), (c_ONESC, d_ONESC), (c_ONESCB, d_ONESCB),
                           (c_ONES64, d_ONES64),
                           (c_convW, d_convW), (c_convB, d_convB), (c_dtb, d_dtb),
                           (c_Alog, d_Alog), (c_DCOL, d_DCOL), (c_nw, d_nw), (c_rw, d_rw)]:
            op("sync", lambda e, a=dst, b=src: e.dma_start(out=a[:], in_=b[:]), inc="lc", inc_n=16)
        wait("scalar", "lc")
        op("scalar", lambda e: e.activation(c_rwh[:], c_rw[:], AF.Copy), inc="se")
        op("scalar", lambda e: e.activation(c_Acol[:], c_Alog[:], AF.Exp), inc="se")
        wait("vector", "se")
        op("vector", lambda e: e.tensor_copy(c_rwstg[:], c_rwh[:]), inc="ve")
        wait("gpsimd", "ve")
        op("gpsimd", lambda e: e.tensor_sub(c_rwl[:], c_rw[:], c_rwstg[:]), inc="ge")
        wait("vector", "lc")
        wait("vector", "se")
        op("vector", lambda e: e.tensor_scalar(c_Acol[:], c_Acol[:], -1.0, None, ALU.mult), inc="ve")
        op("vector", lambda e: e.memset(c_cpool[:, 0:1], EPS))
        op("vector", lambda e: e.memset(c_cpool[:, 1:2], 1.0))
        for rt in range(34):
            sl = rt % 2
            if rt >= 2:
                wait("vector", "dgo")
            for k in range(4):
                op("vector",
                   lambda e, rt=rt, k=k, sl=sl: e.tensor_scalar(
                       dstg[:, sl * 512 + k * 128:sl * 512 + (k + 1) * 128], c_IDBF[:],
                       c_convW[:, rt * 4 + k:rt * 4 + k + 1], None, ALU.mult),
                   inc=("ve" if k == 3 else None))
            wait("sync", "ve")
            op("sync", lambda e, rt=rt, sl=sl: e.dma_start(
                out=d_diag[rt], in_=dstg[:, sl * 512:(sl + 1) * 512]), inc="dgo", inc_n=16)
        wait("sync", "dgo")
        barrier()

        skip[0] = 1 not in phases
        p1_base_skip = [skip[0]]

        def p1s(n):
            skip[0] = p1_base_skip[0] or (n > p1sub)
        with ExitStack() as p1:
            def sb1(name, shape, dt):
                return p1.enter_context(nc.sbuf_tensor(name, shape, dt))
            u2 = [sb1(f"p1u{i}", [128, TT], F32) for i in range(2)]
            usq = [sb1(f"p1usq{i}", [128, TT], F32) for i in range(2)]
            ubf = [sb1(f"p1ub{i}", [128, TT], BF16) for i in range(2)]
            usqbf = [sb1(f"p1uqb{i}", [128, TT], BF16) for i in range(2)]
            xnf = [sb1(f"p1xnf{k}", [128, TT], F32) for k in range(NKD)]
            t1b = sb1("p1t1", [128, TT], F32)
            t2b = sb1("p1t2", [128, TT], F32)
            xnh = [sb1(f"p1xnh{k}", [128, TT], BF16) for k in range(NKD)]
            xnl = [sb1(f"p1xnl{k}", [128, TT], BF16) for k in range(NKD)]
            rowst = sb1("p1rows", [1, 6 * TT], F32)
            rowbf = sb1("p1rowbf", [1, 2 * TT], BF16)
            lg = sb1("p1lg", [128, 32], F32)
            m1c = sb1("p1m1", [128, 4], F32)
            m2c = sb1("p1m2", [128, 4], F32)
            msk1 = sb1("p1msk1", [128, 32], F32)
            msk2 = sb1("p1msk2", [128, 32], F32)
            mskd = sb1("p1mskd", [128, 32], F32)
            d1t = sb1("p1d1t", [128, 32], F32)
            sg1 = sb1("p1sg1", [128, 32], F32)
            t5t = sb1("p1t5t", [128, 32], F32)
            w1c = sb1("p1w1", [128, 4], F32)
            w2c = sb1("p1w2", [128, 4], F32)
            ecol = sb1("p1e", [128, 4], F32)
            ecx = sb1("p1ecx", [128, 4], F32)
            gtm = sb1("p1gtm", [128, 32], F32)
            gtm2 = sb1("p1gtm2", [128, 32], F32)
            gstage = sb1("p1gst", [32, 128], F32)

            P_SUM, P_SQ, P_MBC, P_RBC, P_LG, P_GT = PB[0], PB[1], PB[2], PB[3], PB[4], PB[5]
            R = rowst

            for tt in range(NTT):
                p1s(10)
                for k in range(NKD):
                    sl = u2[k % 2]
                    lus = "lu0" if k % 2 == 0 else "lu1"
                    if k >= 2:
                        wait("sync", "ve")
                    op("sync", lambda e, sl=sl, k=k, tt=tt: e.dma_start(
                        out=sl[:], in_=d_uT[k * 128:(k + 1) * 128, tt * TT:(tt + 1) * TT]),
                        inc=lus, inc_n=16)
                    wait("vector", lus)
                    if k >= 2:
                        wait("vector", "wk")
                    op("vector", lambda e, sl=sl, k=k: e.tensor_mul(usq[k % 2][:], sl[:], sl[:]))
                    op("vector", lambda e, sl=sl, k=k: e.tensor_copy(ubf[k % 2][:], sl[:]), inc="ve")
                    wait("gpsimd", "ve")
                    op("gpsimd", lambda e, k=k: e.tensor_copy(usqbf[k % 2][:], usq[k % 2][:]), inc="ge")
                    wait("tensor", "ge")
                    op("tensor", lambda e, k=k: e.matmul(
                        P_SUM[:1, :], c_ONESCB[:], ubf[k % 2][:], start=(k == 0), stop=(k == NKD - 1)))
                    op("tensor", lambda e, k=k: e.matmul(
                        P_SQ[:1, :], c_ONESCB[:], usqbf[k % 2][:], start=(k == 0), stop=(k == NKD - 1)),
                        inc=("pe" if k == NKD - 1 else "wk"))
                p1s(20)
                wait("vector", "pe")
                op("vector", lambda e: e.tensor_scalar(R[:, 0:TT], P_SUM[:1, :], 1.0 / D, None, ALU.mult))
                op("vector", lambda e: e.tensor_scalar(R[:, TT:2 * TT], P_SQ[:1, :], 1.0 / D, None, ALU.mult))
                op("vector", lambda e: e.tensor_mul(R[:, 5 * TT:6 * TT], R[:, 0:TT], R[:, 0:TT]))
                op("vector", lambda e: e.tensor_sub(R[:, 2 * TT:3 * TT], R[:, TT:2 * TT], R[:, 5 * TT:6 * TT]), inc="ve")
                p1s(21)
                wait("scalar", "ve")
                op("scalar", lambda e: e.activation(R[:, 3 * TT:4 * TT], R[:, 2 * TT:3 * TT], AF.Ln, bias=EPS), inc="se")
                p1s(22)
                wait("vector", "se")
                op("vector", lambda e: e.tensor_scalar(R[:, 5 * TT:6 * TT], R[:, 3 * TT:4 * TT], -0.5, None, ALU.mult), inc="ve")
                wait("scalar", "ve")
                op("scalar", lambda e: e.activation(R[:, 4 * TT:5 * TT], R[:, 5 * TT:6 * TT], AF.Exp), inc="se")
                p1s(23)
                wait("vector", "se")
                op("vector", lambda e: e.tensor_copy(rowbf[:, 0:TT], R[:, 0:TT]))
                op("vector", lambda e: e.tensor_copy(rowbf[:, TT:2 * TT], R[:, 4 * TT:5 * TT]), inc="ve")
                wait("tensor", "ve")
                op("tensor", lambda e: e.matmul(P_MBC[:], c_ONESR[:], rowbf[0:1, 0:TT], start=True, stop=True))
                op("tensor", lambda e: e.matmul(P_RBC[:], c_ONESR[:], rowbf[0:1, TT:2 * TT], start=True, stop=True), inc="pe")
                p1s(30)
                wait("sync", "pe")
                for k in range(NKD):
                    sl = u2[k % 2]
                    lus = "lu0" if k % 2 == 0 else "lu1"
                    if k >= 2:
                        wait("sync", "ve")
                    op("sync", lambda e, sl=sl, k=k, tt=tt: e.dma_start(
                        out=sl[:], in_=d_uT[k * 128:(k + 1) * 128, tt * TT:(tt + 1) * TT]),
                        inc=lus, inc_n=16)
                    wait("vector", lus)
                    op("vector", lambda e, sl=sl: e.tensor_sub(t1b[:], sl[:], P_MBC[:]))
                    op("vector", lambda e, k=k: e.tensor_mul(xnf[k][:], t1b[:], P_RBC[:]))
                    op("vector", lambda e, k=k: e.tensor_copy(xnh[k][:], xnf[k][:]), inc="ve")
                    wait("gpsimd", "ve")
                    op("gpsimd", lambda e, k=k: e.tensor_copy(t2b[:], xnh[k][:]), inc="ge")
                    wait("vector", "ge")
                    op("vector", lambda e, k=k: e.tensor_sub(xnl[k][:], xnf[k][:], t2b[:]), inc="ve")
                    wait("sync", "ve")
                    op("sync", lambda e, k=k, tt=tt: e.dma_start(
                        out=d_xn[k * 128:(k + 1) * 128, tt * TT:(tt + 1) * TT], in_=xnh[k][:]),
                        inc="mwr", inc_n=16)
                p1s(40)
                wait("tensor", "ve")
                for sub in range(TT // 128):
                    for k in range(NKD):
                        op("tensor", lambda e, sub=sub, k=k: e.matmul(
                            P_LG[:, sub * 8:(sub + 1) * 8], xnh[k][:, sub * 128:(sub + 1) * 128],
                            c_rwh[:, k * 8:(k + 1) * 8], start=(k == 0), stop=False))
                        op("tensor", lambda e, sub=sub, k=k: e.matmul(
                            P_LG[:, sub * 8:(sub + 1) * 8], xnh[k][:, sub * 128:(sub + 1) * 128],
                            c_rwl[:, k * 8:(k + 1) * 8], start=False, stop=False))
                        op("tensor", lambda e, sub=sub, k=k: e.matmul(
                            P_LG[:, sub * 8:(sub + 1) * 8], xnl[k][:, sub * 128:(sub + 1) * 128],
                            c_rwh[:, k * 8:(k + 1) * 8], start=False, stop=(k == NKD - 1)),
                            inc=("pe" if (k == NKD - 1 and sub == 3) else None))
                p1s(50)
                def bc4(t):
                    return bass.AP(t if not hasattr(t, "tensor") else t.tensor, 0, [[4, 128], [1, 4], [0, 8]])
                def seg(t):
                    return t[:].rearrange("p (s e) -> p s e", s=4)
                def col4(t):
                    return t[:].rearrange("p (s o) -> p s o", o=1)
                wait("scalar", "pe")
                op("scalar", lambda e: e.activation(lg[:], P_LG[:, 0:32], AF.Copy), inc="se")
                wait("vector", "se")
                op("vector", lambda e: e.reduce_max(col4(m1c), seg(lg), axis=AX.X), inc="ve")
                wait("gpsimd", "ve")
                op("gpsimd", lambda e: e.tensor_tensor(seg(d1t), seg(lg), bc4(m1c), ALU.subtract), inc="ge")
                wait("scalar", "ge")
                op("scalar", lambda e: e.activation(sg1[:], d1t[:], AF.Sign), inc="se")
                wait("gpsimd", "se")
                op("gpsimd", lambda e: e.tensor_scalar(msk1[:], sg1[:], 1.0, None, ALU.add), inc="ge")
                wait("vector", "ge")
                op("vector", lambda e: e.tensor_scalar(t5t[:], msk1[:], -1e30, None, ALU.mult), inc="ve")
                wait("gpsimd", "ve")
                op("gpsimd", lambda e: e.tensor_add(mskd[:], t5t[:], lg[:]), inc="ge")
                wait("vector", "ge")
                op("vector", lambda e: e.reduce_max(col4(m2c), seg(mskd), axis=AX.X), inc="ve")
                wait("gpsimd", "ve")
                op("gpsimd", lambda e: e.tensor_tensor(seg(d1t), seg(mskd), bc4(m2c), ALU.subtract), inc="ge")
                wait("scalar", "ge")
                op("scalar", lambda e: e.activation(sg1[:], d1t[:], AF.Sign), inc="se")
                wait("vector", "se")
                op("vector", lambda e: e.tensor_scalar(msk2[:], sg1[:], 1.0, None, ALU.add))
                op("vector", lambda e: e.tensor_sub(ecol[:], m2c[:], m1c[:]), inc="ve")
                wait("scalar", "ve")
                op("scalar", lambda e: e.activation(ecx[:], ecol[:], AF.Exp), inc="se")
                wait("gpsimd", "se")
                op("gpsimd", lambda e: e.tensor_scalar(w2c[:], ecx[:], 1.0, None, ALU.add), inc="ge")
                wait("vector", "ge")
                op("vector", lambda e: e.reciprocal(w1c[:], w2c[:]), inc="ve")
                wait("gpsimd", "ve")
                op("gpsimd", lambda e: e.tensor_mul(w2c[:], ecx[:], w1c[:]), inc="ge")
                wait("vector", "ge")
                op("vector", lambda e: e.tensor_tensor(seg(gtm2), seg(msk2), bc4(w2c), ALU.mult), inc="ve")
                wait("gpsimd", "ve")
                op("gpsimd", lambda e: e.tensor_tensor(seg(gtm), seg(msk1), bc4(w1c), ALU.mult), inc="ge")
                wait("vector", "ge")
                op("vector", lambda e: e.tensor_add(gtm[:], gtm[:], gtm2[:]), inc="ve")
                p1s(60)
                wait("tensor", "ve")
                op("tensor", lambda e: e.transpose(P_GT[0:32, 0:128], gtm[:], c_IDF32[:]), inc="pe")
                wait("vector", "pe")
                op("vector", lambda e: e.tensor_copy(gstage[:], P_GT[0:32, 0:128]), inc="ve")
                wait("sync", "ve")
                for sub in range(4):
                    op("sync", lambda e, tt=tt, sub=sub: e.dma_start(
                        out=d_g[:, tt * TT + sub * 128:tt * TT + (sub + 1) * 128],
                        in_=gstage[sub * 8:(sub + 1) * 8, :]), inc="gg", inc_n=16)
            p1s(90)
            skip[0] = p1_base_skip[0]
            wait("sync", "gg")
            wait("sync", "mwr")
            barrier()

        skip[0] = 2 not in phases
        p2_base_skip = [skip[0]]

        def p2s(n):
            skip[0] = p2_base_skip[0] or (n > p2sub)
        with ExitStack() as p2:
            def sb2(name, shape, dt):
                return p2.enter_context(nc.sbuf_tensor(name, shape, dt))
            xnc = [sb2(f"p2xn{k}", [128, TT], BF16) for k in range(NKD)]
            wp = [sb2(f"p2wp{i}", [128, 768], BF16) for i in range(3)]
            pre = [sb2(f"p2pre{i}", [128, 516], BF16) for i in range(6)]
            halo = sb2("p2halo", [128, 3 * 34], BF16)
            dgt = [sb2(f"p2dg{i}", [128, 512], BF16) for i in range(2)]
            cvo = [sb2(f"p2cv{i}", [128, TT], BF16) for i in range(34)]
            szst = [sb2(f"p2sz{i}", [128, 32 * 128], BF16) for i in range(2)]
            dtf = sb2("p2dtf", [64, TT], F32)
            sp1 = sb2("p2sp1", [64, TT], F32)
            dAf = sb2("p2dAf", [64, 128], F32)
            acsf = sb2("p2acs", [64, 128], F32)
            acshi = sb2("p2acshi", [64, 128], BF16)
            acshif = sb2("p2acshif", [64, 128], F32)
            acslo = sb2("p2acslo", [64, 128], BF16)
            apack = sb2("p2apack", [1, 16384], BF16)
            negat = sb2("p2negat", [128, 64], F32)
            dttm = sb2("p2dttm", [128, 64], F32)
            wexp = sb2("p2wexp", [128, 64], F32)
            alrow = sb2("p2alrow", [1, 64], F32)
            dkrow = sb2("p2dkrow", [1, 192], F32)
            dkbf = sb2("p2dkbf", [1, 128], BF16)
            dkbc = sb2("p2dkbc", [128, 64], F32)
            mscr = sb2("p2mscr", [128, 128], F32)
            d12 = sb2("p2d12", [128, 128], F32)
            expd = sb2("p2expd", [128, 128], F32)
            eabf = [sb2(f"p2ea{i}", [128, 128], BF16) for i in range(2)]
            mtil = [sb2(f"p2mt{i}", [128, 128], BF16) for i in range(2)]
            ctil = [sb2(f"p2ct{i}", [128, 128], BF16) for i in range(2)]
            xw = [sb2(f"p2xw{i}", [128, 64], BF16) for i in range(2)]
            xtm = sb2("p2xtm", [128, DI], BF16)
            btm = sb2("p2btm", [128, 128], BF16)
            S = sb2("p2S", [128, DI], F32)
            Sbf = sb2("p2Sbf", [128, DI], BF16)
            sstg = sb2("p2sstg", [128, 64], F32)
            yg = sb2("p2yg", [128, DI], BF16)
            ytmp = sb2("p2ytmp", [128, 128], F32)
            ytmp2 = sb2("p2ytmp2", [128, 128], F32)
            ygsq = [sb2(f"p2ygsq{i}", [128, 128], BF16) for i in range(2)]
            rrow = sb2("p2rrow", [1, 128], F32)
            rrow2 = sb2("p2rrow2", [1, 128], F32)
            rbfr = sb2("p2rbfr", [1, 128], BF16)
            ynorm = [sb2(f"p2yn{i}", [128, TT], BF16) for i in range(32)]
            astg = [sb2(f"p2as{i}", [128, 512], F32) for i in range(1)]

            for i8 in range(8):
                op("vector", lambda e, i8=i8: e.memset(S[:, i8 * 512:(i8 + 1) * 512], 0.0),
                   inc=("ve" if i8 == 7 else None))

            NE_TILE = 67
            pidx = [0]
            wslot = [0, 0, 0]

            def wsem(i):
                return ("wA", "wB", "wC")[i]

            def stream_piece(dram, k, c0, ncols):
                i = pidx[0]
                sl = i % 3
                if i >= 3:
                    wait("gpsimd", "wk", cnt["wk"] - 2)
                op("gpsimd", lambda e, sl=sl, k=k, c0=c0, ncols=ncols, dram=dram: e.dma_start(
                    out=wp[sl][:, 0:ncols], in_=dram[k * 128:(k + 1) * 128, c0:c0 + ncols]),
                    inc=wsem(sl), inc_n=16)
                wslot[sl] += 1
                wait("tensor", wsem(sl), 16 * wslot[sl])
                pidx[0] += 1
                return sl

            for tt in range(NTT):
                wait("sync", "wk")
                wait("sync", "pe")
                for k in range(NKD):
                    op("sync", lambda e, k=k, tt=tt: e.dma_start(
                        out=xnc[k][:], in_=d_xn[k * 128:(k + 1) * 128, tt * TT:(tt + 1) * TT]),
                        inc="lx", inc_n=16)
                wait("tensor", "lx")

                groups = []
                et = 0
                while et < NE_TILE:
                    groups.append(list(range(et, min(et + 6, NE_TILE))))
                    et += 6
                prslot = {}
                pre_ring = [0]
                conv_ctr = [0]
                for gi, g in enumerate(groups):
                    p2s(11)
                    wait("tensor", "ve")
                    wait("tensor", "se")
                    for k in range(NKD):
                        c0 = g[0] * 128
                        ncols = sum(64 if e_ == 66 else 128 for e_ in g)
                        sl = stream_piece(d_WinT, k, c0, ncols)
                        col = 0
                        for j, e_ in enumerate(g):
                            w_ = 64 if e_ == 66 else 128
                            tp = (0, 0)
                            op("tensor", lambda e, j=j, sl=sl, col=col, w_=w_, k=k, tp=tp: e.matmul(
                                PB[j][0:w_, :] if w_ == 64 else PB[j][:, :],
                                wp[sl][:, col:col + w_], xnc[k][:],
                                start=(k == 0), stop=(k == NKD - 1), tile_position=tp),
                                inc=("wk" if j == len(g) - 1 else None))
                            col += w_
                    p2s(12)
                    wait("vector", "wk")
                    wait("scalar", "wk")
                    for j, e_ in enumerate(g):
                        if e_ < 32:
                            ssl = e_ % 2
                            if e_ >= 2:
                                wait("scalar", "sz0" if ssl == 0 else "sz1")
                            op("scalar", lambda e, j=j, ssl=ssl: e.activation(
                                dstg[:, ssl * 512:(ssl + 1) * 512], PB[j][:], AF.Silu), inc="se")
                            wait("sync", "se")
                            op("sync", lambda e, e_=e_, ssl=ssl: e.dma_start(
                                out=d_silz[e_], in_=dstg[:, ssl * 512:(ssl + 1) * 512]),
                                inc=("sz0" if ssl == 0 else "sz1"), inc_n=16)
                        elif e_ < 66:
                            rt = e_ - 32
                            ps_ = pre_ring[0] % 6
                            prslot[rt] = ps_
                            pre_ring[0] += 1
                            pr = pre[ps_]
                            if tt == 0:
                                op("vector", lambda e, pr=pr: e.memset(pr[:, 0:3], 0.0))
                            else:
                                op("vector", lambda e, pr=pr, rt=rt: e.tensor_copy(
                                    pr[:, 0:3], halo[:, rt * 3:(rt + 1) * 3]))
                            op("vector", lambda e, pr=pr, j=j: e.tensor_copy(pr[:, 3:515], PB[j][:]))
                            op("vector", lambda e, j=j, rt=rt: e.tensor_copy(
                                halo[:, rt * 3:(rt + 1) * 3], PB[j][:, 509:512]), inc="ve")
                        else:
                            op("vector", lambda e, j=j: e.tensor_copy(dtf[0:64, :], PB[j][0:64, :]), inc="ve")
                    p2s(13)
                    for j, e_ in enumerate(g):
                        if not (32 <= e_ < 66):
                            continue
                        rt = e_ - 32
                        di_ = conv_ctr[0] % 2
                        conv_ctr[0] += 1
                        dcs = "ac0" if di_ == 0 else "ac1"
                        if conv_ctr[0] > 2:
                            wait("sync", "pe")
                        op("sync", lambda e, rt=rt, di_=di_: e.dma_start(
                            out=dgt[di_][:], in_=d_diag[rt]), inc=dcs, inc_n=16)
                        wait("tensor", dcs)
                        wait("tensor", "ve")
                        for kk in range(4):
                            op("tensor", lambda e, j=j, di_=di_, kk=kk, rt=rt: e.matmul(
                                PB[j][:], dgt[di_][:, kk * 128:(kk + 1) * 128],
                                pre[prslot[rt]][:, kk:kk + 512],
                                start=(kk == 0), stop=(kk == 3)),
                                inc=("pe" if kk == 3 else None))
                        wait("scalar", "pe")
                        op("scalar", lambda e, rt=rt, j=j: e.activation(
                            cvo[rt][:], PB[j][:], AF.Silu, bias=c_convB[:, rt:rt + 1]), inc="se")

                p2s(2)
                wait("vector", "ve")
                op("vector", lambda e: e.tensor_scalar(dtf[0:64, :], dtf[0:64, :], c_dtb[:], None, ALU.add), inc="ve")
                wait("scalar", "ve")
                op("scalar", lambda e: e.activation(sp1[:], dtf[0:64, :], AF.Abs))
                op("scalar", lambda e: e.activation(sp1[:], sp1[:], AF.Exp, scale=-1.0))
                op("scalar", lambda e: e.activation(sp1[:], sp1[:], AF.Ln, bias=1.0), inc="se")
                wait("vector", "se")
                op("vector", lambda e: e.tensor_scalar(dtf[0:64, :], dtf[0:64, :], 0.0, None, ALU.max))
                op("vector", lambda e: e.tensor_add(dtf[0:64, :], dtf[0:64, :], sp1[:]), inc="ve")

                p2s(3)
                for cc in range(CPT):
                    p2s(3)
                    col0 = cc * 128
                    ssl = szst[cc % 2]
                    wait("sync", "ve")
                    wait("sync", "sz0")
                    wait("sync", "sz1")
                    szs = "lu0" if cc % 2 == 0 else "lu1"
                    for q in range(4):
                        op("sync", lambda e, ssl=ssl, q=q, col0=col0: e.dma_start(
                            out=ssl[:, q * 8 * 128:(q + 1) * 8 * 128].rearrange("p (t c) -> p t c", t=8),
                            in_=d_silz[q * 8:(q + 1) * 8, :, col0:col0 + 128].rearrange("t p c -> p t c")),
                            inc=szs, inc_n=16)
                    wait("gpsimd", "ve")
                    op("gpsimd", lambda e, col0=col0: e.tensor_scalar(
                        dAf[:], dtf[0:64, col0:col0 + 128], c_Acol[:], None, ALU.mult), inc="ge")
                    wait("vector", "ge")
                    op("vector", lambda e: e.tensor_tensor_scan(
                        acsf[:], c_ONES64[:], dAf[:], 0.0, ALU.mult, ALU.add), inc="ve")
                    wait("scalar", "ve")
                    op("scalar", lambda e: e.activation(acshi[:], acsf[:], AF.Copy), inc="se")
                    wait("vector", "se")
                    op("vector", lambda e: e.tensor_copy(acshif[:], acshi[:]), inc="ve")
                    wait("gpsimd", "ve")
                    op("gpsimd", lambda e: e.tensor_sub(acslo[:], acsf[:], acshif[:]), inc="ge")
                    wait("vector", "ge")
                    wait("sync", "ve")
                    wait("sync", "se")
                    wait("sync", "ge")
                    wait("sync", "pe")
                    wait("sync", "ac1")
                    op("sync", lambda e: e.dma_start(out=d_acs[0], in_=acshi[:]), inc="ac0", inc_n=16)
                    op("sync", lambda e: e.dma_start(out=d_acs[1], in_=acslo[:]), inc="ac0", inc_n=16)
                    wait("sync", "ac0")
                    op("sync", lambda e: e.dma_start(
                        out=apack[:], in_=d_acs[:].rearrange("a h l -> (a h) l").rearrange("(o x) l -> o (x l)", o=1)),
                        inc="ac1", inc_n=16)
                    wait("tensor", "ve")
                    wait("tensor", "se")
                    op("tensor", lambda e: e.transpose(PB[0][0:128, 0:64], acsf[:], c_IDF32[0:64, 0:64]), inc="pe")
                    wait("vector", "pe")
                    op("vector", lambda e: e.tensor_scalar(negat[:], PB[0][:, 0:64], -1.0, None, ALU.mult), inc="ve")
                    wait("tensor", "ve")
                    op("tensor", lambda e, col0=col0: e.transpose(
                        PB[0][0:128, 64:128], dtf[0:64, col0:col0 + 128], c_IDF32[0:64, 0:64]), inc="pe")
                    wait("vector", "pe")
                    op("vector", lambda e: e.tensor_copy(dttm[:], PB[0][:, 64:128]), inc="ve")
                    wait("tensor", "se")
                    wait("tensor", "ve")
                    op("tensor", lambda e, col0=col0: e.transpose(
                        PT[0][:], cvo[32][:, col0:col0 + 128], c_IDBF[:]), inc="pe")
                    wait("vector", "pe")
                    op("vector", lambda e: e.tensor_copy(btm[:], PT[0][:]), inc="ve")
                    for xt_ in range(32):
                        wait("tensor", "ve")
                        op("tensor", lambda e, xt_=xt_, col0=col0: e.transpose(
                            PT[xt_ % 2][:], cvo[xt_][:, col0:col0 + 128], c_IDBF[:]), inc="pe")
                        wait("vector", "pe")
                        op("vector", lambda e, xt_=xt_: e.tensor_copy(
                            xtm[:, xt_ * 128:(xt_ + 1) * 128], PT[xt_ % 2][:]), inc="ve")
                    op("tensor", lambda e, col0=col0: e.matmul(
                        PB[1][:, 0:128], cvo[32][:, col0:col0 + 128], cvo[33][:, col0:col0 + 128],
                        start=True, stop=True), inc="pe")
                    wait("vector", "pe")
                    op("vector", lambda e: e.tensor_mul(mscr[:], PB[1][:, 0:128], c_TRIU[:]), inc="ve")
                    for i8 in range(8):
                        op("vector", lambda e, i8=i8: e.tensor_copy(
                            Sbf[:, i8 * 512:(i8 + 1) * 512], S[:, i8 * 512:(i8 + 1) * 512]),
                            inc=("ve" if i8 == 7 else None))
                    wait("vector", "ac1")
                    op("vector", lambda e: e.tensor_add(alrow[:], apack[0:1, 127:8192:128], apack[0:1, 8319:16384:128]))
                    op("vector", lambda e: e.tensor_copy(dkbf[0:1, 0:64], apack[0:1, 127:8192:128]))
                    op("vector", lambda e: e.tensor_copy(dkbf[0:1, 64:128], apack[0:1, 8319:16384:128]), inc="ve")
                    wait("tensor", "ve")
                    op("tensor", lambda e: e.matmul(PB[0][:, 128:192], c_ONESR[:], dkbf[0:1, 0:64], start=True, stop=False))
                    op("tensor", lambda e: e.matmul(PB[0][:, 128:192], c_ONESR[:], dkbf[0:1, 64:128], start=False, stop=True), inc="pe")
                    wait("vector", "pe")
                    op("vector", lambda e: e.tensor_add(wexp[:], PB[0][:, 128:192], negat[:]), inc="ve")
                    wait("scalar", "ve")
                    op("scalar", lambda e: e.activation(wexp[:], wexp[:], AF.Exp))
                    op("scalar", lambda e: e.activation(dkrow[0:1, 0:64], alrow[:], AF.Exp), inc="se")
                    wait("vector", "se")
                    op("vector", lambda e: e.tensor_mul(wexp[:], wexp[:], dttm[:]))
                    op("vector", lambda e: e.tensor_copy(dkbf[0:1, 0:64], dkrow[0:1, 0:64]), inc="ve")
                    wait("scalar", "ve")
                    op("scalar", lambda e: e.activation(dkrow[0:1, 64:128], dkbf[0:1, 0:64], AF.Copy), inc="se")
                    wait("gpsimd", "se")
                    op("gpsimd", lambda e: e.tensor_sub(dkrow[0:1, 128:192], dkrow[0:1, 0:64], dkrow[0:1, 64:128]), inc="ge")
                    wait("vector", "ge")
                    op("vector", lambda e: e.tensor_copy(dkbf[0:1, 64:128], dkrow[0:1, 128:192]), inc="ve")
                    wait("tensor", "ve")
                    op("tensor", lambda e: e.matmul(PB[0][:, 192:256], c_ONESR[:], dkbf[0:1, 0:64], start=True, stop=False))
                    op("tensor", lambda e: e.matmul(PB[0][:, 192:256], c_ONESR[:], dkbf[0:1, 64:128], start=False, stop=True), inc="pe")
                    wait("vector", "pe")
                    op("vector", lambda e: e.tensor_copy(dkbc[:], PB[0][:, 192:256]), inc="ve")

                    p2s(4)
                    wait("vector", "lu0" if cc % 2 == 0 else "lu1")
                    for hg in range(16):
                        wait("tensor", "se")
                        wait("tensor", "ve")
                        op("tensor", lambda e, hg=hg: e.matmul(
                            PB[3][:], c_ONESR[:], apack[0:1, hg * 512:(hg + 1) * 512], start=True, stop=False))
                        op("tensor", lambda e, hg=hg: e.matmul(
                            PB[3][:], c_ONESR[:], apack[0:1, 8192 + hg * 512:8192 + (hg + 1) * 512], start=False, stop=True), inc="pe")
                        wait("vector", "pe")
                        wait("scalar", "pe")
                        for hh in range(4):
                            h = hg * 4 + hh
                            par = h % 2
                            pair = h // 2
                            ybank = PB[2] if pair % 2 == 0 else PB[5]
                            op("vector", lambda e, hh=hh, h=h: e.tensor_scalar(
                                d12[:], PB[3][:, hh * 128:(hh + 1) * 128], negat[:, h:h + 1], 0.0, ALU.add, ALU.min), inc="ve")
                            wait("scalar", "ve")
                            op("scalar", lambda e: e.activation(expd[:], d12[:], AF.Exp))
                            op("scalar", lambda e, hh=hh, par=par: e.activation(
                                eabf[par][:], PB[3][:, hh * 128:(hh + 1) * 128], AF.Exp), inc="se")
                            wait("vector", "se")
                            op("vector", lambda e, h=h, par=par: e.scalar_tensor_tensor(
                                mtil[par][:], expd[:], dttm[:, h:h + 1], mscr[:], ALU.mult, ALU.mult))
                            op("vector", lambda e, par=par, col0=col0: e.tensor_mul(
                                ctil[par][:], cvo[33][:, col0:col0 + 128], eabf[par][:]))
                            op("vector", lambda e, h=h, par=par: e.tensor_scalar(
                                xw[par][:], xtm[:, h * 64:(h + 1) * 64], wexp[:, h:h + 1], None, ALU.mult), inc="ve")
                            if debug and h < 2:
                                wait("sync", "ve")
                                op("sync", lambda e, tt=tt, cc=cc, h=h, par=par: e.dma_start(
                                    out=d_dbg6[tt * CPT + cc, h], in_=mtil[par][:]), inc="dgo", inc_n=16)
                                wait("vector", "dgo")
                            wait("tensor", "ve")
                            tp = (0, 0) if par == 0 else (0, 64)
                            pr0, pr1 = (0, 64) if par == 0 else (64, 128)
                            if debug != "off":
                                op("tensor", lambda e, h=h, par=par, yb=ybank, tp=tp, pr0=pr0, pr1=pr1: e.matmul(
                                    yb[pr0:pr1, 0:128], xtm[:, h * 64:(h + 1) * 64], mtil[par][:],
                                    start=True, stop=(debug == "diag"), tile_position=tp),
                                    inc=("pe" if debug == "diag" else None))
                            if debug != "diag":
                                op("tensor", lambda e, h=h, par=par, yb=ybank, tp=tp, pr0=pr0, pr1=pr1: e.matmul(
                                    yb[pr0:pr1, 0:128], Sbf[:, h * 64:(h + 1) * 64], ctil[par][:],
                                    start=(debug == "off"), stop=True, tile_position=tp), inc="pe")
                            op("tensor", lambda e, h=h, par=par: e.matmul(
                                PB[4][:, (h % 8) * 64:(h % 8) * 64 + 64], btm[:], xw[par][:],
                                start=True, stop=True), inc="pe")
                            wait("scalar", "pe")
                            op("scalar", lambda e, h=h: e.activation(
                                sstg[:], S[:, h * 64:(h + 1) * 64], AF.Copy, scale=dkbc[:, h:h + 1]), inc="se")
                            wait("vector", "se")
                            wait("vector", "pe")
                            op("vector", lambda e, h=h: e.tensor_add(
                                S[:, h * 64:(h + 1) * 64], sstg[:], PB[4][:, (h % 8) * 64:(h % 8) * 64 + 64]), inc="ve")
                            if par == 1:
                                wait("scalar", "ve")
                                op("scalar", lambda e, pair=pair, col0=col0: e.activation(
                                    ytmp[:], cvo[pair][:, col0:col0 + 128], AF.Copy,
                                    scale=c_DCOL[:, pair:pair + 1]), inc="se")
                                wait("vector", "se")
                                op("vector", lambda e, yb=ybank: e.tensor_add(ytmp2[:], ytmp[:], yb[:, 0:128]), inc="ve")
                                wait("gpsimd", "ve")
                                op("gpsimd", lambda e, ssl=ssl, pair=pair: e.tensor_mul(
                                    yg[:, pair * 128:(pair + 1) * 128], ytmp2[:], ssl[:, pair * 128:(pair + 1) * 128]), inc="ge")
                                if debug and pair == 0:
                                    op("vector", lambda e, yb=ybank: e.tensor_copy(ytmp[:], yb[:, 0:128]), inc="ve")
                                    wait("sync", "ve")
                                    op("sync", lambda e, tt=tt, cc=cc: e.dma_start(
                                        out=d_dbg7[tt * CPT + cc], in_=ytmp[:]), inc="dgo", inc_n=16)
                                    wait("vector", "dgo")
                    if debug:
                        wait("sync", "ge")
                        wait("sync", "ve")
                        ci = tt * CPT + cc
                        op("sync", lambda e, ci=ci: e.dma_start(out=d_dbg[ci], in_=yg[:]), inc="dgo", inc_n=16)
                        op("sync", lambda e, ci=ci: e.dma_start(out=d_dbg2[ci], in_=xtm[:]), inc="dgo", inc_n=16)
                        op("sync", lambda e, ci=ci: e.dma_start(out=d_dbg3[ci][:, 0:64], in_=negat[:]), inc="dgo", inc_n=16)
                        op("sync", lambda e, ci=ci: e.dma_start(out=d_dbg3[ci][:, 64:128], in_=dttm[:]), inc="dgo", inc_n=16)
                        op("sync", lambda e, ci=ci: e.dma_start(out=d_dbg3[ci][:, 128:192], in_=wexp[:]), inc="dgo", inc_n=16)
                        op("sync", lambda e, ci=ci: e.dma_start(out=d_dbg3[ci][:, 192:256], in_=dkbc[:]), inc="dgo", inc_n=16)
                        op("sync", lambda e, ci=ci: e.dma_start(out=d_dbg4[ci], in_=mscr[:]), inc="dgo", inc_n=16)
                        op("sync", lambda e, ci=ci: e.dma_start(out=d_dbg5[ci], in_=apack[:]), inc="dgo", inc_n=16)
                        wait("vector", "dgo")
                        wait("tensor", "dgo")
                    p2s(5)
                    wait("vector", "ge")
                    for t32 in range(32):
                        if t32 >= 2:
                            wait("vector", "wk")
                        op("vector", lambda e, t32=t32: e.tensor_mul(
                            ygsq[t32 % 2][:], yg[:, t32 * 128:(t32 + 1) * 128], yg[:, t32 * 128:(t32 + 1) * 128]), inc="ve")
                        wait("tensor", "ve")
                        op("tensor", lambda e, t32=t32: e.matmul(
                            PB[0][0:1, 256:384], c_ONESCB[:], ygsq[t32 % 2][:],
                            start=(t32 == 0), stop=(t32 == 31)), inc=("pe" if t32 == 31 else "wk"))
                    wait("scalar", "pe")
                    op("scalar", lambda e: e.activation(rrow[:], PB[0][0:1, 256:384], AF.Ln, scale=1.0 / DI, bias=EPS), inc="se")
                    wait("vector", "se")
                    op("vector", lambda e: e.tensor_scalar(rrow2[:], rrow[:], -0.5, None, ALU.mult), inc="ve")
                    wait("scalar", "ve")
                    op("scalar", lambda e: e.activation(rrow[:], rrow2[:], AF.Exp), inc="se")
                    wait("gpsimd", "se")
                    op("gpsimd", lambda e: e.tensor_copy(rbfr[:], rrow[:]), inc="ge")
                    wait("tensor", "ge")
                    op("tensor", lambda e: e.matmul(PB[1][:, 128:256], c_ONESR[:], rbfr[:], start=True, stop=True), inc="pe")
                    wait("vector", "pe")
                    for t32 in range(32):
                        op("vector", lambda e, t32=t32, col0=col0: e.scalar_tensor_tensor(
                            ynorm[t32][:, col0:col0 + 128], yg[:, t32 * 128:(t32 + 1) * 128],
                            c_nw[:, t32:t32 + 1], PB[1][:, 128:256], ALU.mult, ALU.mult),
                            inc=("ve" if t32 == 31 else None))

                p2s(6)
                wgrp = [list(range(0, 6)), list(range(6, 12)), list(range(12, 16))]
                for gi, g in enumerate(wgrp):
                    c0 = g[0] * 128
                    ncols = len(g) * 128
                    wait("tensor", "ve")
                    wait("tensor", "se")
                    for k in range(32):
                        sl = stream_piece(d_WoutT, k, c0, ncols)
                        for j, dm in enumerate(g):
                            op("tensor", lambda e, j=j, sl=sl, k=k: e.matmul(
                                PB[j][:], wp[sl][:, j * 128:(j + 1) * 128], ynorm[k][:],
                                start=(k == 0), stop=(k == 31)),
                                inc=("wk" if j == len(g) - 1 else None))
                    wait("vector", "wk")
                    for j, dm in enumerate(g):
                        asl = astg[0]
                        uasem = "ua0"
                        wait("vector", uasem)
                        op("vector", lambda e, j=j, asl=asl: e.tensor_copy(asl[:], PB[j][:]), inc="ve")
                        wait("sync", "ve")
                        op("sync", lambda e, dm=dm, tt=tt, asl=asl: e.dma_start(
                            out=d_attn[dm * 128:(dm + 1) * 128, tt * TT:(tt + 1) * TT], in_=asl[:]),
                            inc=uasem, inc_n=16)
            p2s(99)
            skip[0] = p2_base_skip[0]
            wait("sync", "ua0")
            wait("sync", "ua1")
            barrier()

        skip[0] = 3 not in phases
        with ExitStack() as p3:
            def sb3(name, shape, dt):
                return p3.enter_context(nc.sbuf_tensor(name, shape, dt))
            xnc3 = [sb3(f"p3xn{k}", [128, TT2], BF16) for k in range(NKD)]
            ffn = [sb3(f"p3f{i}", [128, TT2], F32) for i in range(16)]
            act = [sb3(f"p3a{i}", [128, TT2], BF16) for i in range(NIP)]
            w1s2 = [sb3(f"p3w1s{i}", [128, 16 * 512], BF16) for i in range(2)]
            w2s2 = [sb3(f"p3w2s{i}", [128, 11 * 256], BF16) for i in range(2)]
            w1guard = [0, 0]
            w2guard = [0, 0]
            w1ctr = [0]
            w2ctr = [0]
            gst = sb3("p3gst", [1, TT2], F32)
            gbf = sb3("p3gbf", [1, TT2], BF16)
            gbc = sb3("p3gbc", [128, TT2], F32)
            silt = sb3("p3silt", [128, 512], F32)
            t2s = sb3("p3t2", [128, 512], F32)
            ustg = [sb3(f"p3u{i}", [128, TT2], F32) for i in range(2)]
            astg3 = [sb3(f"p3at{i}", [128, TT2], F32) for i in range(2)]

            for t2 in range(NTT2):
                wait("sync", "pe")
                for k in range(NKD):
                    op("sync", lambda e, k=k, t2=t2: e.dma_start(
                        out=xnc3[k][:], in_=d_xn[k * 128:(k + 1) * 128, t2 * TT2:(t2 + 1) * TT2]),
                        inc="lx", inc_n=16)
                for dm in range(16):
                    us, as_ = ustg[dm % 2], astg3[dm % 2]
                    uasem = "ua0" if dm % 2 == 0 else "ua1"
                    if dm >= 2:
                        wait("sync", "ve")
                    op("sync", lambda e, dm=dm, t2=t2, us=us: e.dma_start(
                        out=us[:], in_=d_uT[dm * 128:(dm + 1) * 128, t2 * TT2:(t2 + 1) * TT2]),
                        inc=uasem, inc_n=16)
                    op("sync", lambda e, dm=dm, t2=t2, as_=as_: e.dma_start(
                        out=as_[:], in_=d_attn[dm * 128:(dm + 1) * 128, t2 * TT2:(t2 + 1) * TT2]),
                        inc=uasem, inc_n=16)
                    wait("vector", uasem)
                    op("vector", lambda e, dm=dm, us=us, as_=as_: e.tensor_add(ffn[dm][:], us[:], as_[:]), inc="ve")
                wait("tensor", "lx")
                for e_ in range(E):
                    if not (t2 == 0 and e_ == 0):
                        wait("sync", "ve")
                    op("sync", lambda e, e_=e_, t2=t2: e.dma_start(
                        out=gst[:], in_=d_g[e_:e_ + 1, t2 * TT2:(t2 + 1) * TT2]), inc="gg", inc_n=16)
                    wait("vector", "gg")
                    op("vector", lambda e: e.tensor_copy(gbf[:], gst[:]), inc="ve")
                    wait("tensor", "ve")
                    wait("tensor", "pe", cnt["pe"])
                    op("tensor", lambda e: e.matmul(PB[0][:], c_ONESR[:], gbf[0:1, 0:512], start=True, stop=True))
                    op("tensor", lambda e: e.matmul(PB[1][:], c_ONESR[:], gbf[0:1, 512:1024], start=True, stop=True), inc="pe")
                    wait("vector", "pe")
                    op("vector", lambda e: e.tensor_copy(gbc[:, 0:512], PB[0][:]))
                    op("vector", lambda e: e.tensor_copy(gbc[:, 512:1024], PB[1][:]), inc="ve")
                    bank_rot = [0]
                    mm1_hist = []
                    mm1_base = cnt["ve"]
                    for pg2 in range(6):
                        np_ = 2 if pg2 < 5 else 1
                        cw = np_ * 256
                        wsl = w1ctr[0] % 2
                        w1ctr[0] += 1
                        w1s = w1s2[wsl]
                        wait("gpsimd", "pe", w1guard[wsl])
                        for q in range(4):
                            op("gpsimd", lambda e, e_=e_, pg2=pg2, q=q, cw=cw, w1s=w1s: e.dma_start(
                                out=w1s[:, q * 4 * cw:(q + 1) * 4 * cw].rearrange("p (k c) -> p k c", k=4),
                                in_=d_w1[e_, q * 512:(q + 1) * 512, pg2 * 512:pg2 * 512 + cw].rearrange(
                                    "(k p) c -> p k c", p=128)),
                                inc="sA", inc_n=16)
                        wait("tensor", "sA")
                        for pp in range(np_):
                            pair = pg2 * 2 + pp
                            for hf in range(2):
                                bg = PB[bank_rot[0] % 6]
                                bu_ = PB[(bank_rot[0] + 1) % 6]
                                uidx = bank_rot[0] // 2
                                bank_rot[0] += 2
                                wait("tensor", "ve",
                                     mm1_hist[uidx - 3] if uidx >= 3 else mm1_base)
                                for k in range(NKD):
                                    op("tensor", lambda e, k=k, pp=pp, cw=cw, bg=bg, hf=hf, w1s=w1s: e.matmul(
                                        bg[:], w1s[:, k * cw + pp * 256:k * cw + pp * 256 + 128],
                                        xnc3[k][:, hf * 512:(hf + 1) * 512],
                                        start=(k == 0), stop=(k == NKD - 1)))
                                    op("tensor", lambda e, k=k, pp=pp, cw=cw, bu_=bu_, hf=hf, w1s=w1s: e.matmul(
                                        bu_[:], w1s[:, k * cw + pp * 256 + 128:k * cw + pp * 256 + 256],
                                        xnc3[k][:, hf * 512:(hf + 1) * 512],
                                        start=(k == 0), stop=(k == NKD - 1)),
                                        inc=("pe" if k == NKD - 1 else None))
                                wait("scalar", "pe")
                                op("scalar", lambda e, bg=bg: e.activation(silt[:], bg[:], AF.Silu), inc="se")
                                wait("vector", "se")
                                op("vector", lambda e, bu_=bu_: e.tensor_mul(t2s[:], bu_[:], silt[:]))
                                op("vector", lambda e, pair=pair, hf=hf: e.tensor_mul(
                                    act[pair][:, hf * 512:(hf + 1) * 512], t2s[:], gbc[:, hf * 512:(hf + 1) * 512]), inc="ve")
                                mm1_hist.append(cnt["ve"])
                        w1guard[wsl] = cnt["pe"]
                    for dmg in range(8):
                        wsl2 = w2ctr[0] % 2
                        w2ctr[0] += 1
                        w2s = w2s2[wsl2]
                        wait("gpsimd", "pe", w2guard[wsl2])
                        for q in range(2):
                            rows = 768 if q == 0 else 640
                            nki = 6 if q == 0 else 5
                            op("gpsimd", lambda e, e_=e_, dmg=dmg, q=q, rows=rows, nki=nki, w2s=w2s: e.dma_start(
                                out=w2s[:, q * 6 * 256:q * 6 * 256 + nki * 256].rearrange("p (k c) -> p k c", k=nki),
                                in_=d_w2[e_, q * 768:q * 768 + rows, dmg * 256:(dmg + 1) * 256].rearrange(
                                    "(k p) c -> p k c", p=128)),
                                inc="sB", inc_n=16)
                        wait("tensor", "sB")
                        wait("tensor", "ve")
                        br = [0]
                        for dd in range(2):
                            dm = dmg * 2 + dd
                            for hf in range(2):
                                bk = PB[br[0] % 6]
                                br[0] += 1
                                for ki in range(NIP):
                                    op("tensor", lambda e, ki=ki, dd=dd, bk=bk, hf=hf, w2s=w2s: e.matmul(
                                        bk[:], w2s[:, ki * 256 + dd * 128:ki * 256 + dd * 128 + 128],
                                        act[ki][:, hf * 512:(hf + 1) * 512],
                                        start=(ki == 0), stop=(ki == NIP - 1)),
                                        inc=("pe" if ki == NIP - 1 else None))
                                wait("vector", "pe")
                                op("vector", lambda e, dm=dm, hf=hf, bk=bk: e.tensor_add(
                                    ffn[dm][:, hf * 512:(hf + 1) * 512],
                                    ffn[dm][:, hf * 512:(hf + 1) * 512], bk[:]), inc="ve")
                        w2guard[wsl2] = cnt["pe"]
                wait("sync", "ve")
                for dm in range(16):
                    op("sync", lambda e, dm=dm, t2=t2: e.dma_start(
                        out=d_out[dm * 128:(dm + 1) * 128, t2 * TT2:(t2 + 1) * TT2], in_=ffn[dm][:]),
                        inc="so", inc_n=16)
            wait("sync", "so")

        skip[0] = False
        if 3 not in phases:
            op("vector", lambda e: e.tensor_copy(dstg[:, 0:512], c_IDF32[:, 0:128].rearrange("p (a b) -> p (a b)", a=1) if False else c_IDF32[:]) if False else e.tensor_copy(c_TRIU[:], c_IDF32[:]), inc="ve")
            wait("sync", "ve")
            op("sync", lambda e: e.dma_start(out=d_out[0:128, 0:128], in_=c_TRIU[:]), inc="so", inc_n=16)
            wait("sync", "so")

        block = ctx.enter_context(nc.Block())

        @block.sync
        def _(eng):
            for f in streams["sync"]:
                f(eng)

        @block.gpsimd
        def _(eng):
            for f in streams["gpsimd"]:
                f(eng)

        @block.vector
        def _(eng):
            for f in streams["vector"]:
                f(eng)

        @block.scalar
        def _(eng):
            for f in streams["scalar"]:
                f(eng)

        @block.tensor
        def _(eng):
            for f in streams["tensor"]:
                f(eng)

    return nc


def host_layout(inputs):
    u = np.asarray(inputs["L0_attn_u"])
    W_in = np.asarray(inputs["L0_attn_W_in"])
    conv_w = np.asarray(inputs["L0_attn_conv_w"])
    conv_b = np.asarray(inputs["L0_attn_conv_b"])
    dt_bias = np.asarray(inputs["L0_attn_dt_bias"])
    A_log = np.asarray(inputs["L0_attn_A_log"])
    Dh = np.asarray(inputs["L0_attn_D"])
    nw = np.asarray(inputs["L0_attn_norm_weight"])
    W_out = np.asarray(inputs["L0_attn_W_out"])
    rw = np.asarray(inputs["L0_ffn_router_w"])
    w1 = np.asarray(inputs["L0_ffn_w1"])
    w2 = np.asarray(inputs["L0_ffn_w2"])

    WinT = np.ascontiguousarray(W_in.T)
    convW = np.ascontiguousarray(conv_w.reshape(34, 128, 4).transpose(1, 0, 2).reshape(128, 136))
    convB = np.ascontiguousarray(conv_b.reshape(34, 128).T)
    dtb = np.ascontiguousarray(dt_bias.reshape(64, 1))
    Alog = np.ascontiguousarray(A_log.reshape(64, 1))
    DCOL = np.ascontiguousarray(
        np.concatenate([np.broadcast_to(Dh[0::2], (64, 32)), np.broadcast_to(Dh[1::2], (64, 32))], axis=0))
    nwc = np.ascontiguousarray(nw.reshape(32, 128).T)
    WoutT = np.ascontiguousarray(W_out.T)
    rwT = np.ascontiguousarray(rw.T.reshape(16, 128, 8).transpose(1, 0, 2).reshape(128, 128))
    w1T = w1.transpose(0, 2, 1)
    w1p = np.empty_like(w1T)
    for j in range(NIP):
        w1p[:, :, j * 256:j * 256 + 128] = w1T[:, :, j * 128:(j + 1) * 128]
        w1p[:, :, j * 256 + 128:j * 256 + 256] = w1T[:, :, 1408 + j * 128:1408 + (j + 1) * 128]
    w1p = np.ascontiguousarray(w1p)
    w2p = np.ascontiguousarray(w2.transpose(0, 2, 1))

    consts = {
        "idbf": np.eye(128, dtype=bfnp),
        "idf32": np.eye(128, dtype=np.float32),
        "triu": np.ascontiguousarray(np.triu(np.ones((128, 128), dtype=np.float32))),
        "onesr": np.ones((1, 128), dtype=bfnp),
        "onesc": np.ones((128, 1), dtype=np.float32),
        "onescb": np.ones((128, 1), dtype=bfnp),
        "ones64": np.ones((64, 128), dtype=np.float32),
    }
    shared = {"WinT": WinT, "convW": convW, "convB": convB, "dtb": dtb,
              "Alog": Alog, "DCOL": DCOL, "nwc": nwc, "WoutT": WoutT,
              "rw": rwT, "w1p": w1p, "w2p": w2p}
    shared.update(consts)
    in_maps = []
    for c in range(8):
        m = dict(shared)
        m["uT"] = np.ascontiguousarray(u[c].T)
        in_maps.append(m)
    return in_maps


def kernel(**inputs):
    nc = build_nc()
    in_maps = host_layout(inputs)
    res = run_bass_kernel_spmd(nc, in_maps, core_ids=list(range(8)))
    out = np.stack([np.ascontiguousarray(np.asarray(r["outT"]).T) for r in res.results], axis=0)
    return out


if __name__ == "__main__":
    nc = build_nc()
    print("build ok")
